# version 01
# baseline (speedup 1.0000x reference)
import contextlib

import numpy as np
import ml_dtypes

import concourse.bass as bass
import concourse.mybir as mybir
from concourse.bass_utils import run_bass_kernel_spmd

F32 = mybir.dt.float32
BF16 = mybir.dt.bfloat16
AF = mybir.ActivationFunctionType
ALU = mybir.AluOpType
AX = mybir.AxisListType.X

P = 128
D = 2048
L = 4096
H = 16
ND = D // P
NT = L // P
NTC = L // 512
SCALE = float(1.0 / np.sqrt(128.0))
EPS = 1e-5
NCORES = 8


PH_ORDER = ["ln1", "q", "k", "v", "bias", "attn", "o", "f1", "f23"]


def build_nc(taps=(), upto="f23"):
    import os as _os
    _last = PH_ORDER.index(upto)
    _skip = set(x for x in _os.environ.get("SKIP_PH", "").split(",") if x)

    def ph(name):
        return PH_ORDER.index(name) <= _last and name not in _skip

    nc = bass.Bass("TRN2", target_bir_lowering=False)

    x_in = nc.dram_tensor("x", [L, D], F32, kind="ExternalInput")
    wqkvT = nc.dram_tensor("wqkvT", [D, 3 * D], F32, kind="ExternalInput")
    woT = nc.dram_tensor("woT", [D, D], F32, kind="ExternalInput")
    wupT = nc.dram_tensor("wupT", [D, D], F32, kind="ExternalInput")
    wdownT = nc.dram_tensor("wdownT", [D, D], F32, kind="ExternalInput")
    wexp = nc.dram_tensor("wexp", [D, D], F32, kind="ExternalInput")
    wgT = nc.dram_tensor("wgT", [D, 8], F32, kind="ExternalInput")
    costab = nc.dram_tensor("costab", [P, L], F32, kind="ExternalInput")
    nsintab = nc.dram_tensor("nsintab", [P, L], F32, kind="ExternalInput")
    identin = nc.dram_tensor("identin", [P, P], BF16, kind="ExternalInput")
    masksin = nc.dram_tensor("masksin", [P, 4 * 512], BF16, kind="ExternalInput")
    out = nc.dram_tensor("out", [L, D], F32, kind="ExternalOutput")

    h1d = nc.dram_tensor("h1d", [D, L], BF16)
    qd = nc.dram_tensor("qd", [D, L], BF16)
    kd = nc.dram_tensor("kd", [D, L], BF16)
    vt = nc.dram_tensor("vt", [L, D], BF16)
    otok = nc.dram_tensor("otok", [L, D], BF16)
    x2d = nc.dram_tensor("x2d", [L, D], F32)
    h2d = nc.dram_tensor("h2d", [D, L], BF16)
    upd = nc.dram_tensor("upd", [D, L], BF16)
    scr_map = {"h1d": h1d, "qd": qd, "kd": kd, "vt": vt, "otok": otok,
               "x2d": x2d, "h2d": h2d, "upd": upd}
    tap_outs = [(nc.dram_tensor("tap_" + n, list(scr_map[n].shape),
                                scr_map[n].dtype, kind="ExternalOutput"),
                 scr_map[n]) for n in taps]

    with contextlib.ExitStack() as ctx:
        e = ctx.enter_context

        ident = e(nc.sbuf_tensor("ident", [P, P], BF16))
        wg_sb = e(nc.sbuf_tensor("wg_sb", [P, ND * 8], BF16))
        ones_sb = e(nc.sbuf_tensor("ones_sb", [P, P], BF16))
        qn2 = e(nc.sbuf_tensor("qn2", [1, H], F32))
        kn2 = e(nc.sbuf_tensor("kn2", [1, H], F32))
        biasC = e(nc.sbuf_tensor("biasC", [P, H], F32))
        rw_all = e(nc.sbuf_tensor("rw_all", [P, NT], F32))
        smallA = e(nc.sbuf_tensor("smallA", [1, 16], F32))
        smallB = e(nc.sbuf_tensor("smallB", [1, 16], BF16))

        WB0 = e(nc.sbuf_tensor("WB0", [P, ND * D], BF16))
        WB1 = e(nc.sbuf_tensor("WB1", [P, ND * D], BF16))

        S = {}
        for name in [
            "sWB0", "sWB1", "sCst", "gOnes",
            "sX", "vH", "vvL", "aaL", "pTL", "vCpL", "sH1w",
            "sh1Q", "pMMQ", "vPQ", "vStQ", "vSqQ", "pQNQ", "vQNQ", "sQw", "vvQ", "pTCQ",
            "sh1K", "pMMK", "vPK", "vStK", "vSqK", "pQNK", "vQNK", "sKw", "vvK", "pTCK",
            "sh1V", "pMMV", "aVV", "sVw",
            "vvB", "aaB", "pB",
            "sKV", "sQl", "pS", "aExp", "vMask", "pPt", "pOd", "vOev", "sOw",
            "pQd", "pHd", "vvA", "sMsk",
            "sOl", "pT1", "aCp1", "pWo", "vEvO", "pLg", "vGt", "sX2w",
            "sH2w", "vvO", "aaO", "vH2",
            "sh2F", "pF1", "aRl", "sUw", "pTCF",
            "sF3l", "pF2h", "vF23", "sOutW",
            "sTap",
        ]:
            S[name] = e(nc.semaphore(name))

        def wload(g, dst, src, src_ld, col0, ncols, sem):
            for part in range(4):
                dst_ap = dst[:, :].rearrange("p (c m) -> p c m", c=ND)[
                    :, 4 * part:4 * part + 4, 0:ncols]
                src_ap = bass.AP(
                    src, (4 * part) * P * src_ld + col0,
                    [[src_ld, P], [P * src_ld, 4], [1, ncols]])
                g.dma_start(dst_ap, src_ap).then_inc(sem, 16)

        with contextlib.ExitStack() as lc:
            el = lc.enter_context
            Xt = el(nc.sbuf_tensor("XtL", [P, 2 * D], F32))
            scr = el(nc.sbuf_tensor("scrL", [P, D], F32))
            hbf = el(nc.sbuf_tensor("hbfL", [P, 2 * D], BF16))
            h1st = el(nc.sbuf_tensor("h1stL", [P, 2 * D], BF16))
            lnst = el(nc.sbuf_tensor("lnstL", [P, 16], F32))
            tp = [el(nc.psum_tensor("tpA", [P, P], BF16)),
                  el(nc.psum_tensor("tpB", [P, P], BF16))]

            with nc.Block() as blk:
                @blk.gpsimd
                def _(g):
                    g.memset(ones_sb[:, :], 1.0)
                    g.memset(qn2[:, :], 0.0)
                    g.memset(kn2[:, :], 0.0).then_inc(S["gOnes"], 1)
                    wload(g, WB0, wqkvT, 3 * D, 0, D, S["sWB0"])
                    wload(g, WB1, wqkvT, 3 * D, D, D, S["sWB1"])
                    g.dma_start(
                        wg_sb[:, :].rearrange("p (c m) -> p c m", c=ND),
                        bass.AP(wgT, 0, [[8, P], [P * 8, ND], [1, 8]]),
                    ).then_inc(S["sCst"], 16)
                    for t in range(NT):
                        b = t % 2
                        g.wait_ge(S["vCpL"], ND * (t + 1))
                        g.dma_start(
                            bass.AP(h1d, t * P, [[L, P], [P * L, ND], [1, P]]),
                            h1st[:, :].rearrange("p (b c m) -> p b c m",
                                                 b=2, c=ND)[:, b],
                        ).then_inc(S["sH1w"], 16)

                @blk.sync
                def _(s):
                    s.dma_start(ident[:, :], identin[:, :]).then_inc(S["sCst"], 16)
                    for t in range(NT):
                        if t >= 2:
                            s.wait_ge(S["vH"], t - 1)
                        s.dma_start(
                            Xt[:, (t % 2) * D:(t % 2) * D + D],
                            bass.AP(x_in, t * P * D, [[D, P], [1, D]]),
                        ).then_inc(S["sX"], 16)

                def emit_ln_copies(v, t):
                    b = t % 2
                    if t >= 2:
                        v.wait_ge(S["sH1w"], 16 * (t - 1))
                    for c in range(ND):
                        i = t * ND + c
                        v.wait_ge(S["pTL"], i + 1)
                        v.tensor_copy(h1st[:, b * D + c * P:b * D + c * P + P],
                                      tp[i % 2][:, :]).then_inc(S["vCpL"], 1)

                @blk.vector
                def _(v):
                    n = 0
                    for t in range(NT):
                        b = t % 2
                        xa = Xt[:, b * D:b * D + D]
                        v.wait_ge(S["sX"], 16 * (t + 1))
                        if t >= 2:
                            v.wait_ge(S["pTL"], ND * (t - 1))
                        v.scalar_tensor_tensor(
                            scr[:, :], xa, 1.0, xa, op0=ALU.mult, op1=ALU.mult,
                            accum_out=lnst[:, 1:2]).then_inc(S["vvL"], 1)
                        v.reduce_sum(lnst[:, 0:1], xa, axis=AX
                                     ).then_inc(S["vvL"], 1)
                        n += 2
                        v.wait_ge(S["vvL"], n)
                        v.tensor_scalar_mul(lnst[:, 2:3], lnst[:, 0:1], 1.0 / D
                                            ).then_inc(S["vvL"], 1)
                        v.tensor_scalar(lnst[:, 3:4], lnst[:, 1:2], 1.0 / D,
                                        EPS, op0=ALU.mult, op1=ALU.add
                                        ).then_inc(S["vvL"], 1)
                        n += 2
                        v.wait_ge(S["vvL"], n)
                        v.tensor_mul(lnst[:, 4:5], lnst[:, 2:3], lnst[:, 2:3]
                                     ).then_inc(S["vvL"], 1)
                        n += 1
                        v.wait_ge(S["vvL"], n)
                        v.tensor_sub(lnst[:, 5:6], lnst[:, 3:4], lnst[:, 4:5]
                                     ).then_inc(S["vvL"], 1)
                        n += 1
                        v.wait_ge(S["aaL"], 2 * (t + 1))
                        v.tensor_scalar(lnst[:, 8:9], lnst[:, 2:3],
                                        lnst[:, 7:8], -1.0,
                                        op0=ALU.mult, op1=ALU.mult
                                        ).then_inc(S["vvL"], 1)
                        n += 1
                        v.wait_ge(S["vvL"], n)
                        v.tensor_scalar(
                            hbf[:, b * D:b * D + D], xa,
                            lnst[:, 7:8], lnst[:, 8:9],
                            op0=ALU.mult, op1=ALU.add).then_inc(S["vH"], 1)
                        if t >= 1:
                            emit_ln_copies(v, t - 1)
                    emit_ln_copies(v, NT - 1)

                @blk.scalar
                def _(sc):
                    for t in range(NT):
                        sc.wait_ge(S["vvL"], 7 * t + 6)
                        sc.activation(lnst[:, 6:7], lnst[:, 5:6], AF.Ln
                                      ).then_inc(S["aaL"], 1)
                        sc.wait_ge(S["aaL"], 2 * t + 1)
                        sc.activation(lnst[:, 7:8], lnst[:, 6:7], AF.Exp,
                                      scale=-0.5).then_inc(S["aaL"], 1)

                @blk.tensor
                def _(tn):
                    tn.wait_ge(S["sCst"], 32)
                    for t in range(NT):
                        b = t % 2
                        tn.wait_ge(S["vH"], t + 1)
                        for c in range(ND):
                            i = t * ND + c
                            if i >= 2:
                                tn.wait_ge(S["vCpL"], i - 1)
                            tn.transpose(
                                tp[i % 2][:, :],
                                hbf[:, b * D + c * P:b * D + c * P + P],
                                ident[:, :]).then_inc(S["pTL"], 1)

        def qk_phase(tag, col0, WB, sWBn, wbtgt, sh1, pMM, vP, vSt, vSq, pQN,
                     vQN, sW, pTC, stats, dst, prefetch):
            with contextlib.ExitStack() as qc_:
                eq = qc_.enter_context
                h1c = eq(nc.sbuf_tensor("h1c" + tag, [P, 2 * ND * 512], BF16))
                cosS = eq(nc.sbuf_tensor("cosS" + tag, [P, 2 * 512], F32))
                nsinS = eq(nc.sbuf_tensor("nsinS" + tag, [P, 2 * 512], F32))
                ropeA = eq(nc.sbuf_tensor("ropeA" + tag, [P, 512], F32))
                ropeB = eq(nc.sbuf_tensor("ropeB" + tag, [P, 512], F32))
                qst = eq(nc.sbuf_tensor("qst" + tag, [P, 2 * 512], BF16))
                sqb = eq(nc.sbuf_tensor("sqb" + tag, [P, 2 * 512], BF16))
                qp = [eq(nc.psum_tensor("qpA" + tag, [P, 512], F32)),
                      eq(nc.psum_tensor("qpB" + tag, [P, 512], F32))]
                qn = [eq(nc.psum_tensor("qnA" + tag, [1, 512], F32)),
                      eq(nc.psum_tensor("qnB" + tag, [1, 512], F32))]
                NI = NTC * H

                with nc.Block() as blk:
                    @blk.sync
                    def _(s):
                        for tc in range(NTC):
                            if tc >= 2:
                                s.wait_ge(pMM, H * (tc - 1))
                            s.wait_ge(S["sH1w"], 16 * 4 * (tc + 1))
                            b = tc % 2
                            s.dma_start(
                                h1c[:, :].rearrange("p (b c m) -> p b c m",
                                                    b=2, c=ND)[:, b],
                                bass.AP(h1d, tc * 512,
                                        [[L, P], [P * L, ND], [1, 512]]),
                            ).then_inc(sh1, 16)
                            s.dma_start(cosS[:, b * 512:b * 512 + 512],
                                        costab[:, tc * 512:tc * 512 + 512]
                                        ).then_inc(sh1, 16)
                            s.dma_start(nsinS[:, b * 512:b * 512 + 512],
                                        nsintab[:, tc * 512:tc * 512 + 512]
                                        ).then_inc(sh1, 16)

                    def norm_mm(tn, i):
                        tn.wait_ge(vSq, i + 1)
                        if i >= 2:
                            tn.wait_ge(vQN, i - 1)
                        tn.matmul(qn[i % 2][:, :], ones_sb[:, 0:1],
                                  sqb[:, (i % 2) * 512:(i % 2) * 512 + 512],
                                  start=True, stop=True).then_inc(pQN, 1)

                    @blk.tensor
                    def _(tn):
                        tn.wait_ge(sWBn, wbtgt)
                        tn.wait_ge(S["gOnes"], 1)
                        for tc in range(NTC):
                            tn.wait_ge(sh1, 48 * (tc + 1))
                            b = tc % 2
                            for ee in range(H):
                                i = tc * H + ee
                                if i >= 2:
                                    tn.wait_ge(vP, i - 1)
                                for d in range(ND):
                                    mm = tn.matmul(
                                        qp[i % 2][:, :],
                                        WB[:, d * D + ee * P:d * D + ee * P + P],
                                        h1c[:, b * ND * 512 + d * 512:
                                            b * ND * 512 + d * 512 + 512],
                                        start=(d == 0), stop=(d == ND - 1))
                                mm.then_inc(pMM, 1)
                                if i >= 1:
                                    norm_mm(tn, i - 1)
                        norm_mm(tn, NI - 1)

                    def qk_norm_red(v, j, nv):
                        eej = j % H
                        v.wait_ge(pQN, j + 1)
                        v.reduce_max(smallA[0:1, 0:1], qn[j % 2][0:1, :],
                                     axis=AX).then_inc(S["vv" + tag], 1)
                        nv += 1
                        v.wait_ge(S["vv" + tag], nv)
                        v.tensor_max(stats[0:1, eej:eej + 1],
                                     stats[0:1, eej:eej + 1],
                                     smallA[0:1, 0:1]).then_inc(vQN, 1)
                        return nv

                    @blk.vector
                    def _(v):
                        nv = 0
                        for tc in range(NTC):
                            b = tc % 2
                            for ee in range(H):
                                i = tc * H + ee
                                sp = i % 2
                                v.wait_ge(pMM, i + 1)
                                cs = cosS[:, b * 512:b * 512 + 512]
                                ns = nsinS[:, b * 512:b * 512 + 512]
                                v.tensor_mul(ropeA[:, :], qp[sp][:, :], cs)
                                v.tensor_mul(ropeB[0:64, :],
                                             qp[sp][64:128, :], ns[0:64, :])
                                v.tensor_mul(ropeB[64:128, :],
                                             qp[sp][0:64, :], ns[64:128, :]
                                             ).then_inc(vP, 1)
                                if i >= 2:
                                    v.wait_ge(sW, 16 * (i - 1))
                                v.tensor_add(qst[:, sp * 512:sp * 512 + 512],
                                             ropeA[:, :], ropeB[:, :]
                                             ).then_inc(vSt, 1)
                                if i >= 2:
                                    v.wait_ge(pQN, i - 1)
                                v.tensor_mul(sqb[:, sp * 512:sp * 512 + 512],
                                             qst[:, sp * 512:sp * 512 + 512],
                                             qst[:, sp * 512:sp * 512 + 512]
                                             ).then_inc(vSq, 1)
                                if i >= 1:
                                    nv = qk_norm_red(v, i - 1, nv)
                        qk_norm_red(v, NI - 1, nv)

                    @blk.scalar
                    def _(sc):
                        for tc in range(NTC):
                            for ee in range(H):
                                i = tc * H + ee
                                sp = i % 2
                                sc.wait_ge(vSt, i + 1)
                                sc.dma_start(
                                    bass.AP(dst, (ee * P) * L + tc * 512,
                                            [[L, P], [1, 512]]),
                                    qst[:, sp * 512:sp * 512 + 512],
                                ).then_inc(sW, 16)

                    @blk.gpsimd
                    def _(g):
                        if prefetch is not None:
                            pf_dst, pf_src, pf_ld, pf_col, pf_sem = prefetch
                            wload(g, pf_dst, pf_src, pf_ld, pf_col, D, pf_sem)

        if ph("q"):
            qk_phase("Q", 0, WB0, S["sWB0"], 64, S["sh1Q"], S["pMMQ"], S["vPQ"],
                     S["vStQ"], S["vSqQ"], S["pQNQ"], S["vQNQ"], S["sQw"],
                     S["pTCQ"], qn2, qd, None)
        if ph("k"):
            qk_phase("K", D, WB1, S["sWB1"], 64, S["sh1K"], S["pMMK"], S["vPK"],
                     S["vStK"], S["vSqK"], S["pQNK"], S["vQNK"], S["sKw"],
                     S["pTCK"], kn2, kd,
                     (WB0, wqkvT, 3 * D, 2 * D, S["sWB0"]))

        if ph("v"):
         with contextlib.ExitStack() as vc_:
            ev2 = vc_.enter_context
            h1cV = ev2(nc.sbuf_tensor("h1cV", [P, 2 * ND * 512], BF16))
            vst = ev2(nc.sbuf_tensor("vst", [P, 2 * 512], BF16))
            vp = [ev2(nc.psum_tensor("vpA", [P, 512], F32)),
                  ev2(nc.psum_tensor("vpB", [P, 512], F32))]

            with nc.Block() as blk:
                @blk.sync
                def _(s):
                    for tc in range(NTC):
                        if tc >= 2:
                            s.wait_ge(S["pMMV"], 16 * (tc - 1))
                        b = tc % 2
                        s.dma_start(
                            h1cV[:, :].rearrange("p (b c m) -> p b c m",
                                                 b=2, c=ND)[:, b],
                            bass.AP(h1d, tc * 512,
                                    [[L, P], [P * L, ND], [1, 512]]),
                        ).then_inc(S["sh1V"], 16)

                @blk.tensor
                def _(tn):
                    tn.wait_ge(S["sWB0"], 128)
                    for tc in range(NTC):
                        tn.wait_ge(S["sh1V"], 16 * (tc + 1))
                        b = tc % 2
                        for t4 in range(4):
                            for e4 in range(4):
                                i = (tc * 4 + t4) * 4 + e4
                                if i >= 2:
                                    tn.wait_ge(S["aVV"], i - 1)
                                for d in range(ND):
                                    mm = tn.matmul(
                                        vp[i % 2][:, :],
                                        h1cV[:, b * ND * 512 + d * 512 + t4 * P:
                                             b * ND * 512 + d * 512 + t4 * P + P],
                                        WB0[:, d * D + e4 * 512:
                                            d * D + e4 * 512 + 512],
                                        start=(d == 0), stop=(d == ND - 1))
                                mm.then_inc(S["pMMV"], 1)

                @blk.scalar
                def _(sc):
                    for i in range(NTC * 16):
                        tc, rem = divmod(i, 16)
                        t4, e4 = divmod(rem, 4)
                        sp = i % 2
                        sc.wait_ge(S["pMMV"], i + 1)
                        if i >= 2:
                            sc.wait_ge(S["sVw"], 16 * (i - 1))
                        sc.copy(vst[:, sp * 512:sp * 512 + 512], vp[sp][:, :]
                                ).then_inc(S["aVV"], 1)
                        sc.wait_ge(S["aVV"], i + 1)
                        sc.dma_start(
                            bass.AP(vt, (tc * 512 + t4 * P) * D + e4 * 512,
                                    [[D, P], [1, 512]]),
                            vst[:, sp * 512:sp * 512 + 512],
                        ).then_inc(S["sVw"], 16)

                @blk.gpsimd
                def _(g):
                    wload(g, WB1, woT, D, 0, D, S["sWB1"])

        if ph("bias"):
         with contextlib.ExitStack() as bc_:
            bps = bc_.enter_context(nc.psum_tensor("bps", [P, H], F32))
            with nc.Block() as blk:
                @blk.vector
                def _(v):
                    v.tensor_mul(smallA[0:1, 0:16], qn2[0:1, :], kn2[0:1, :]
                                 ).then_inc(S["vvB"], 1)
                    v.wait_ge(S["vvB"], 1)
                    v.tensor_copy(smallB[0:1, 0:16], smallA[0:1, 0:16]
                                  ).then_inc(S["vvB"], 1)
                    v.wait_ge(S["aaB"], 2)
                    v.tensor_scalar(biasC[:, :], biasC[:, :], -SCALE, -1.0,
                                    op0=ALU.mult, op1=ALU.add
                                    ).then_inc(S["vvB"], 1)

                @blk.tensor
                def _(tn):
                    tn.wait_ge(S["vvB"], 2)
                    for h in range(H):
                        mm = tn.matmul(bps[:, h:h + 1], ones_sb[0:1, :],
                                       smallB[0:1, h:h + 1], start=True,
                                       stop=True)
                    mm.then_inc(S["pB"], 1)

                @blk.scalar
                def _(sc):
                    sc.wait_ge(S["pB"], 1)
                    sc.activation(biasC[:, :], bps[:, :], AF.Ln
                                  ).then_inc(S["aaB"], 1)
                    sc.wait_ge(S["aaB"], 1)
                    sc.activation(biasC[:, :], biasC[:, :], AF.Exp, scale=0.5
                                  ).then_inc(S["aaB"], 1)

        import os
        _astage = int(os.environ.get("ATT_STAGE", "4"))
        _njq = int(os.environ.get("ATT_J", str(H * NTC)))
        ahq = [(h, qc) for h in range(H) for qc in range(NTC)][:_njq]
        apairs = []
        pair_of = {}
        diag_ord = {}
        for j, (h, qc) in enumerate(ahq):
            for pp in range(2 * qc + 2):
                pair_of[(j, pp)] = len(apairs)
                apairs.append((j, h, qc, pp, pp >= 2 * qc))
        ndi = 0
        for g, (j, h, qc, pp, diag) in enumerate(apairs):
            if diag:
                ndi += 1
                diag_ord[g] = ndi
        cum_pairs = [0] * len(ahq)
        _acc = 0
        for j, (h, qc) in enumerate(ahq):
            _acc += 2 * qc + 2
            cum_pairs[j] = _acc

        VW = 132
        if ph("attn"):
         with contextlib.ExitStack() as ac_:
            ea = ac_.enter_context
            Kb = ea(nc.sbuf_tensor("Kb", [P, 2 * L], BF16))
            Vb = ea(nc.sbuf_tensor("Vb", [P, 2 * 32 * VW], BF16))
            Qb = ea(nc.sbuf_tensor("Qb", [P, 2 * 512], BF16))
            Pt = ea(nc.sbuf_tensor("Pt", [P, 2 * 1024], BF16))
            masks_sb = ea(nc.sbuf_tensor("masks_sb", [P, 4 * 512], BF16))
            biasX = ea(nc.sbuf_tensor("biasX", [P, H], F32))
            Otst = ea(nc.sbuf_tensor("Otst", [P, 2 * 512], BF16))
            rcA = ea(nc.sbuf_tensor("rcA", [P, 8], F32))
            OtF = ea(nc.sbuf_tensor("OtF", [P, 4 * 132], F32))
            Sps = [ea(nc.psum_tensor("SpsA", [P, 1024], F32)),
                   ea(nc.psum_tensor("SpsB", [P, 1024], F32))]
            Ops = [ea(nc.psum_tensor("Ops%d" % qs, [P, 512], F32))
                   for qs in range(4)]

            with nc.Block() as blk:
                @blk.gpsimd
                def _(g):
                    g.memset(Vb[:, :].rearrange("p (n c) -> p n c", c=VW)
                             [:, :, 128:129], 1.0).then_inc(S["gOnes"], 1)
                    wload(g, WB0, wupT, D, 0, D, S["sWB0"])
                    for j, (h, qc) in enumerate(ahq):
                        if _astage < 4 or float(os.environ.get("ATT_EVAC", "3")) < 3:
                            break
                        g.wait_ge(S["vOev"], j + 1)
                        g.dma_start(
                            bass.AP(otok, (qc * 512) * D + h * P,
                                    [[D, P], [P * D, 4], [1, P]]),
                            Otst[:, :].rearrange("p (b q m) -> p b q m",
                                                 b=2, q=4)[:, j % 2],
                        ).then_inc(S["sOw"], 16)

                _nheads = (len(ahq) + NTC - 1) // NTC
                @blk.sync
                def _(s):
                    s.dma_start(masks_sb[:, :], masksin[:, :]
                                ).then_inc(S["sMsk"], 16)
                    if ph("k"):
                        s.wait_ge(S["sKw"], 16 * H * NTC)
                    if ph("v"):
                        s.wait_ge(S["sVw"], 16 * NTC * 16)
                    if ph("q"):
                        s.wait_ge(S["sQw"], 16 * H * NTC)
                    for h in range(_nheads):
                        hb = h % 2
                        if h >= 2:
                            s.wait_ge(S["pPt"], cum_pairs[(h - 1) * NTC - 1])
                        s.dma_start(
                            Kb[:, hb * L:hb * L + L],
                            bass.AP(kd, (h * P) * L, [[L, P], [1, L]]),
                        ).then_inc(S["sKV"], 16)
                        s.dma_start(
                            Vb[:, :].rearrange("p (b n c) -> p b n c",
                                               b=2, n=32)[:, hb, :, 0:128],
                            bass.AP(vt, h * P, [[D, P], [P * D, 32], [1, P]]),
                        ).then_inc(S["sKV"], 16)
                        for qc in range(NTC):
                            j = h * NTC + qc
                            if j >= len(ahq):
                                break
                            if j >= 2:
                                s.wait_ge(S["pS"], cum_pairs[j - 2])
                            s.dma_start(
                                Qb[:, (j % 2) * 512:(j % 2) * 512 + 512],
                                bass.AP(qd, (h * P) * L + qc * 512,
                                        [[L, P], [1, 512]]),
                            ).then_inc(S["sQl"], 16)

                def emit_pv(tn, j, h, qc, pp):
                    if _astage < 3:
                        return
                    g = pair_of[(j, pp)]
                    sp = g % 2
                    hb = h % 2
                    npair = 2 * qc + 2
                    last = pp == npair - 1
                    if _astage >= 2 and pp >= 2 * qc:
                        tn.wait_ge(S["vMask"], diag_ord[g])
                    else:
                        tn.wait_ge(S["aExp"], g + 1)
                    if _astage >= 4 and pp == 0 and j >= 1:
                        tn.wait_ge(S["vOev"], j)
                    for half in range(2):
                        kb = 2 * pp + half
                        for qs in range(4):
                            mm = tn.matmul(
                                Ops[qs][:, 0:129],
                                Pt[:, sp * 1024 + half * 512 + qs * P:
                                   sp * 1024 + half * 512 + qs * P + P],
                                Vb[:, hb * 32 * VW + kb * VW:
                                   hb * 32 * VW + kb * VW + 129],
                                start=(kb == 0), stop=(kb == 2 * npair - 1))
                    mm.then_inc(S["pPt"], 1)

                @blk.tensor
                def _(tn):
                    tn.wait_ge(S["gOnes"], (2 if ph("ln1") else 1))
                    for j, (h, qc) in enumerate(ahq):
                        hb = h % 2
                        tn.wait_ge(S["sKV"], 32 * (h + 1))
                        tn.wait_ge(S["sQl"], 16 * (j + 1))
                        npair = 2 * qc + 2
                        for pp in range(npair):
                            g = pair_of[(j, pp)]
                            sp = g % 2
                            if _astage >= 1 and g >= 2:
                                tn.wait_ge(S["aExp"], g - 1)
                            for half in range(2):
                                kb = 2 * pp + half
                                mm = tn.matmul(
                                    Sps[sp][:, half * 512:half * 512 + 512],
                                    Kb[:, hb * L + kb * P:hb * L + kb * P + P],
                                    Qb[:, (j % 2) * 512:(j % 2) * 512 + 512],
                                    start=True, stop=True)
                            mm.then_inc(S["pS"], 1)
                            if pp >= 1:
                                emit_pv(tn, j, h, qc, pp - 1)
                        emit_pv(tn, j, h, qc, npair - 1)

                @blk.scalar
                def _(sc):
                    for g, (j, h, qc, pp, diag) in enumerate(apairs):
                        if _astage < 1:
                            break
                        sp = g % 2
                        sc.wait_ge(S["pS"], g + 1)
                        if _astage >= 3 and g >= 2:
                            sc.wait_ge(S["pPt"], g - 1)
                        _bias = (0.0 if os.environ.get("ATT_NOBIAS")
                                 else biasX[:, h:h + 1])
                        if g == 0:
                            sc.wait_ge(S["vvA"], 1)
                        sc.activation(Pt[:, sp * 1024:sp * 1024 + 1024],
                                      Sps[sp][:, :], AF.Exp,
                                      bias=_bias, scale=SCALE
                                      ).then_inc(S["aExp"], 1)

                @blk.vector
                def _(v):
                    v.tensor_copy(biasX[:, :], biasC[:, :]).then_inc(S["vvA"], 1)
                    v.wait_ge(S["sMsk"], 16)
                    nva = 1
                    for j, (h, qc) in enumerate(ahq):
                        if _astage >= 2:
                            for pp in (2 * qc, 2 * qc + 1):
                                g = pair_of[(j, pp)]
                                sp = g % 2
                                v.wait_ge(S["aExp"], g + 1)
                                for half in range(2):
                                    jd = 2 * (pp - 2 * qc) + half
                                    mul = v.tensor_mul(
                                        Pt[:, sp * 1024 + half * 512:
                                           sp * 1024 + half * 512 + 512],
                                        Pt[:, sp * 1024 + half * 512:
                                           sp * 1024 + half * 512 + 512],
                                        masks_sb[:, jd * 512:jd * 512 + 512])
                                mul.then_inc(S["vMask"], 1)
                        if _astage < 4:
                            continue
                        _evac = float(os.environ.get("ATT_EVAC", "3"))
                        v.wait_ge(S["pPt"], cum_pairs[j])
                        if _evac >= 3 and j >= 2:
                            v.wait_ge(S["sOw"], 16 * (j - 1))
                        if _evac >= 1.1:
                            for qs in range(4):
                                v.tensor_copy(OtF[:, qs * 132:qs * 132 + 129],
                                              Ops[qs][:, 0:129]
                                              ).then_inc(S["vvA"], 1)
                                nva += 1
                            v.wait_ge(S["vvA"], nva)
                        if _evac >= 1.2:
                            for qs in range(4):
                                v.reciprocal(rcA[:, qs:qs + 1],
                                             OtF[:, qs * 132 + 128:qs * 132 + 129]
                                             ).then_inc(S["vvA"], 1)
                                nva += 1
                            v.wait_ge(S["vvA"], nva)
                        if _evac >= 2:
                            for qs in range(4):
                                ts = v.tensor_scalar_mul(
                                    Otst[:, (j % 2) * 512 + qs * P:
                                         (j % 2) * 512 + qs * P + P],
                                    OtF[:, qs * 132:qs * 132 + P],
                                    rcA[:, qs:qs + 1])
                            ts.then_inc(S["vOev"], 1)
                        else:
                            v.nop().then_inc(S["vOev"], 1)

        if ph("o"):
         with contextlib.ExitStack() as oc_:
            eo = oc_.enter_context
            Otile = eo(nc.sbuf_tensor("Otile", [P, 2 * D], BF16))
            XtO = eo(nc.sbuf_tensor("XtO", [P, 2 * D], F32))
            x2t = eo(nc.sbuf_tensor("x2t", [P, 2 * D], F32))
            h2b = eo(nc.sbuf_tensor("h2b", [P, 2 * D], BF16))
            OfmT = eo(nc.sbuf_tensor("OfmT", [P, 2 * D], BF16))
            h2T = eo(nc.sbuf_tensor("h2T", [P, 2 * D], BF16))
            lnstO = eo(nc.sbuf_tensor("lnstO", [P, 16], F32))
            lgsb = eo(nc.sbuf_tensor("lgsb", [P, 8], F32))
            gtmp = eo(nc.sbuf_tensor("gtmp", [P, 24], F32))
            tpO = [eo(nc.psum_tensor("tpOA", [P, P], BF16)),
                   eo(nc.psum_tensor("tpOB", [P, P], BF16))]
            attn_ps = [eo(nc.psum_tensor("attnA", [P, 1024], F32)),
                       eo(nc.psum_tensor("attnB", [P, 1024], F32))]
            lgps = eo(nc.psum_tensor("lgps", [P, 8], F32))

            vvo_ln = []
            vvo_gate = []

            with nc.Block() as blk:
                @blk.sync
                def _(s):
                    s.wait_ge(S["sOw"], 16 * H * NTC)
                    for t in range(NT):
                        b = t % 2
                        if t >= 2:
                            s.wait_ge(S["pT1"], 32 * (t - 2) + 16)
                            s.wait_ge(S["vEvO"], 2 * (t - 2) + 2)
                        s.dma_start(
                            Otile[:, b * D:b * D + D],
                            bass.AP(otok, t * P * D, [[D, P], [1, D]]),
                        ).then_inc(S["sOl"], 16)
                        s.dma_start(
                            XtO[:, b * D:b * D + D],
                            bass.AP(x_in, t * P * D, [[D, P], [1, D]]),
                        ).then_inc(S["sOl"], 16)

                @blk.tensor
                def _(tn):
                    tn.wait_ge(S["sWB1"], 128)
                    for t in range(NT):
                        b = t % 2
                        tn.wait_ge(S["sOl"], 32 * (t + 1))
                        for c in range(ND):
                            i = t * 32 + c
                            if i >= 2:
                                tn.wait_ge(S["aCp1"], i - 1)
                            tn.transpose(tpO[i % 2][:, :],
                                         Otile[:, b * D + c * P:b * D + c * P + P],
                                         ident[:, :]).then_inc(S["pT1"], 1)
                        for half in range(2):
                            if half == 0:
                                tn.wait_ge(S["aCp1"], t * 32 + ND)
                            if t >= 1:
                                tn.wait_ge(S["vEvO"], 2 * (t - 1) + half + 1)
                            for d in range(ND):
                                for sl in range(2):
                                    mm = tn.matmul(
                                        attn_ps[half][:, sl * 512:sl * 512 + 512],
                                        OfmT[:, b * D + d * P:b * D + d * P + P],
                                        WB1[:, d * D + half * 1024 + sl * 512:
                                            d * D + half * 1024 + sl * 512 + 512],
                                        start=(d == 0), stop=(d == ND - 1))
                            mm.then_inc(S["pWo"], 1)
                        tn.wait_ge(S["vH2"], t + 1)
                        for c in range(ND):
                            i = t * 32 + 16 + c
                            tn.wait_ge(S["aCp1"], i - 1)
                            tn.transpose(tpO[i % 2][:, :],
                                         h2b[:, b * D + c * P:b * D + c * P + P],
                                         ident[:, :]).then_inc(S["pT1"], 1)
                        tn.wait_ge(S["aCp1"], 32 * (t + 1))
                        if t >= 1:
                            tn.wait_ge(S["vGt"], t)
                        for d in range(ND):
                            mm = tn.matmul(lgps[:, :],
                                           h2T[:, b * D + d * P:b * D + d * P + P],
                                           wg_sb[:, d * 8:d * 8 + 8],
                                           start=(d == 0), stop=(d == ND - 1))
                        mm.then_inc(S["pLg"], 1)

                @blk.vector
                def _(v):
                    nvo = 0
                    for t in range(NT):
                        b = t % 2
                        for half in range(2):
                            v.wait_ge(S["pWo"], 2 * t + half + 1)
                            if half == 0 and t >= 2:
                                v.wait_ge(S["sX2w"], 16 * (t - 1))
                            add = v.tensor_add(
                                x2t[:, b * D + half * 1024:b * D + half * 1024 + 1024],
                                XtO[:, b * D + half * 1024:b * D + half * 1024 + 1024],
                                attn_ps[half][:, :])
                            if half == 0:
                                add.then_inc(S["vEvO"], 1)
                        x2a = x2t[:, b * D:b * D + D]
                        v.scalar_tensor_tensor(
                            OfmT[:, b * D:b * D + D], x2a, 1.0, x2a,
                            op0=ALU.mult, op1=ALU.mult,
                            accum_out=lnstO[:, 1:2]).then_inc(S["vvO"], 1)
                        v.reduce_sum(lnstO[:, 0:1], x2a, axis=AX
                                     ).then_inc(S["vvO"], 1)
                        nvo += 2
                        v.wait_ge(S["vvO"], nvo)
                        v.nop().then_inc(S["vEvO"], 1)
                        v.tensor_scalar_mul(lnstO[:, 2:3], lnstO[:, 0:1], 1.0 / D
                                            ).then_inc(S["vvO"], 1)
                        v.tensor_scalar(lnstO[:, 3:4], lnstO[:, 1:2], 1.0 / D,
                                        EPS, op0=ALU.mult, op1=ALU.add
                                        ).then_inc(S["vvO"], 1)
                        nvo += 2
                        v.wait_ge(S["vvO"], nvo)
                        v.tensor_mul(lnstO[:, 4:5], lnstO[:, 2:3],
                                     lnstO[:, 2:3]).then_inc(S["vvO"], 1)
                        nvo += 1
                        v.wait_ge(S["vvO"], nvo)
                        v.tensor_sub(lnstO[:, 5:6], lnstO[:, 3:4],
                                     lnstO[:, 4:5]).then_inc(S["vvO"], 1)
                        nvo += 1
                        vvo_ln.append(nvo)
                        v.wait_ge(S["aaO"], 3 * t + 2)
                        v.tensor_scalar(lnstO[:, 8:9], lnstO[:, 2:3],
                                        lnstO[:, 7:8], -1.0,
                                        op0=ALU.mult, op1=ALU.mult
                                        ).then_inc(S["vvO"], 1)
                        nvo += 1
                        v.wait_ge(S["vvO"], nvo)
                        if t >= 2:
                            v.wait_ge(S["pT1"], 32 * (t - 1))
                        v.tensor_scalar(h2b[:, b * D:b * D + D], x2a,
                                        lnstO[:, 7:8], lnstO[:, 8:9],
                                        op0=ALU.mult, op1=ALU.add
                                        ).then_inc(S["vH2"], 1)
                        v.wait_ge(S["pLg"], t + 1)
                        v.tensor_copy(lgsb[:, :], lgps[:, :]).then_inc(S["vvO"], 1)
                        nvo += 1
                        v.wait_ge(S["vvO"], nvo)
                        v.nop().then_inc(S["vGt"], 1)
                        v.reduce_max(gtmp[:, 0:1], lgsb[:, :], axis=AX
                                     ).then_inc(S["vvO"], 1)
                        nvo += 1
                        v.wait_ge(S["vvO"], nvo)
                        v.tensor_scalar(gtmp[:, 8:16], lgsb[:, :],
                                        gtmp[:, 0:1], None, op0=ALU.subtract
                                        ).then_inc(S["vvO"], 1)
                        nvo += 1
                        v.wait_ge(S["vvO"], nvo)
                        v.tensor_scalar(gtmp[:, 16:24], gtmp[:, 8:16], 0.0,
                                        None, op0=ALU.is_ge
                                        ).then_inc(S["vvO"], 1)
                        nvo += 1
                        v.wait_ge(S["vvO"], nvo)
                        v.scalar_tensor_tensor(gtmp[:, 8:16], gtmp[:, 16:24],
                                               -1e30, gtmp[:, 8:16],
                                               op0=ALU.mult, op1=ALU.add
                                               ).then_inc(S["vvO"], 1)
                        nvo += 1
                        v.wait_ge(S["vvO"], nvo)
                        v.reduce_max(gtmp[:, 1:2], gtmp[:, 8:16], axis=AX
                                     ).then_inc(S["vvO"], 1)
                        nvo += 1
                        vvo_gate.append(nvo)
                        v.wait_ge(S["aaO"], 3 * t + 3)
                        v.tensor_scalar_add(gtmp[:, 3:4], gtmp[:, 2:3], 1.0
                                            ).then_inc(S["vvO"], 1)
                        nvo += 1
                        v.wait_ge(S["vvO"], nvo)
                        v.reciprocal(gtmp[:, 4:5], gtmp[:, 3:4]
                                     ).then_inc(S["vvO"], 1)
                        nvo += 1
                        v.wait_ge(S["vvO"], nvo)
                        v.tensor_mul(gtmp[:, 5:6], gtmp[:, 2:3], gtmp[:, 4:5]
                                     ).then_inc(S["vvO"], 1)
                        nvo += 1
                        v.wait_ge(S["vvO"], nvo)
                        v.tensor_add(rw_all[:, t:t + 1], gtmp[:, 4:5],
                                     gtmp[:, 5:6])

                @blk.scalar
                def _(sc):
                    for t in range(NT):
                        b = t % 2
                        if t >= 2:
                            sc.wait_ge(S["pWo"], 2 * (t - 2) + 2)
                        for c in range(ND):
                            i = t * 32 + c
                            sc.wait_ge(S["pT1"], i + 1)
                            sc.copy(OfmT[:, b * D + c * P:b * D + c * P + P],
                                    tpO[i % 2][:, :]).then_inc(S["aCp1"], 1)
                        sc.wait_ge(S["vvO"], vvo_ln[t])
                        sc.activation(lnstO[:, 6:7], lnstO[:, 5:6], AF.Ln
                                      ).then_inc(S["aaO"], 1)
                        sc.wait_ge(S["aaO"], 3 * t + 1)
                        sc.activation(lnstO[:, 7:8], lnstO[:, 6:7], AF.Exp,
                                      scale=-0.5).then_inc(S["aaO"], 1)
                        if t >= 2:
                            sc.wait_ge(S["sH2w"], 16 * (t - 1))
                            sc.wait_ge(S["pLg"], t - 1)
                        for c in range(ND):
                            i = t * 32 + 16 + c
                            sc.wait_ge(S["pT1"], i + 1)
                            sc.copy(h2T[:, b * D + c * P:b * D + c * P + P],
                                    tpO[i % 2][:, :]).then_inc(S["aCp1"], 1)
                        sc.wait_ge(S["vvO"], vvo_gate[t])
                        sc.activation(gtmp[:, 2:3], gtmp[:, 1:2], AF.Exp
                                      ).then_inc(S["aaO"], 1)

                @blk.gpsimd
                def _(g):
                    for t in range(NT):
                        b = t % 2
                        g.wait_ge(S["vEvO"], 2 * t + 2)
                        g.dma_start(
                            bass.AP(x2d, t * P * D, [[D, P], [1, D]]),
                            x2t[:, b * D:b * D + D]).then_inc(S["sX2w"], 16)
                        g.wait_ge(S["aCp1"], 32 * (t + 1))
                        g.dma_start(
                            bass.AP(h2d, t * P, [[L, P], [P * L, ND], [1, P]]),
                            h2T[:, :].rearrange("p (b c m) -> p b c m",
                                                b=2, c=ND)[:, b],
                        ).then_inc(S["sH2w"], 16)

        if ph("f1"):
         with contextlib.ExitStack() as fc_:
            ef = fc_.enter_context
            h2cF = ef(nc.sbuf_tensor("h2cF", [P, 2 * ND * 512], BF16))
            upst = ef(nc.sbuf_tensor("upst", [P, 2 * ND * 512], BF16))
            upps = [ef(nc.psum_tensor("upA", [P, 512], F32)),
                    ef(nc.psum_tensor("upB", [P, 512], F32))]

            with nc.Block() as blk:
                @blk.sync
                def _(s):
                    s.wait_ge(S["sH2w"], 16 * NT)
                    for tc in range(NTC):
                        if tc >= 2:
                            s.wait_ge(S["pF1"], ND * (tc - 1))
                        b = tc % 2
                        s.dma_start(
                            h2cF[:, :].rearrange("p (b c m) -> p b c m",
                                                 b=2, c=ND)[:, b],
                            bass.AP(h2d, tc * 512,
                                    [[L, P], [P * L, ND], [1, 512]]),
                        ).then_inc(S["sh2F"], 16)

                @blk.tensor
                def _(tn):
                    tn.wait_ge(S["sWB0"], 192)
                    for tc in range(NTC):
                        tn.wait_ge(S["sh2F"], 16 * (tc + 1))
                        b = tc % 2
                        for ee in range(ND):
                            i = tc * ND + ee
                            if i >= 2:
                                tn.wait_ge(S["aRl"], i - 1)
                            for d in range(ND):
                                mm = tn.matmul(
                                    upps[i % 2][:, :],
                                    WB0[:, d * D + ee * P:d * D + ee * P + P],
                                    h2cF[:, b * ND * 512 + d * 512:
                                         b * ND * 512 + d * 512 + 512],
                                    start=(d == 0), stop=(d == ND - 1))
                            mm.then_inc(S["pF1"], 1)

                @blk.scalar
                def _(sc):
                    for tc in range(NTC):
                        b = tc % 2
                        if tc >= 2:
                            sc.wait_ge(S["sUw"], 16 * (tc - 1))
                        for ee in range(ND):
                            i = tc * ND + ee
                            sc.wait_ge(S["pF1"], i + 1)
                            sc.activation(
                                upst[:, b * ND * 512 + ee * 512:
                                     b * ND * 512 + ee * 512 + 512],
                                upps[i % 2][:, :], AF.Relu).then_inc(S["aRl"], 1)

                @blk.gpsimd
                def _(g):
                    wload(g, WB1, wdownT, D, 0, D, S["sWB1"])
                    for tc in range(NTC):
                        b = tc % 2
                        g.wait_ge(S["aRl"], ND * (tc + 1))
                        g.dma_start(
                            bass.AP(upd, tc * 512,
                                    [[L, P], [P * L, ND], [1, 512]]),
                            upst[:, :].rearrange("p (b c m) -> p b c m",
                                                 b=2, c=ND)[:, b],
                        ).then_inc(S["sUw"], 16)

        if ph("f23"):
         with contextlib.ExitStack() as gc_:
            eg = gc_.enter_context
            uptile = eg(nc.sbuf_tensor("uptile", [P, 2 * D], BF16))
            h2tile = eg(nc.sbuf_tensor("h2tile", [P, 2 * D], BF16))
            x2tile = eg(nc.sbuf_tensor("x2tile", [P, 2 * D], F32))
            outt = eg(nc.sbuf_tensor("outt", [P, 2 * D], F32))
            ftmp = eg(nc.sbuf_tensor("ftmp", [P, 1024], F32))
            sh_ps = [eg(nc.psum_tensor("shA", [P, 1024], F32)),
                     eg(nc.psum_tensor("shB", [P, 1024], F32))]
            ex_ps = [eg(nc.psum_tensor("exA", [P, 1024], F32)),
                     eg(nc.psum_tensor("exB", [P, 1024], F32))]

            with nc.Block() as blk:
                @blk.gpsimd
                def _(g):
                    wload(g, WB0, wexp, D, 0, D, S["sWB0"])

                @blk.sync
                def _(s):
                    s.wait_ge(S["sUw"], 16 * NTC)
                    s.wait_ge(S["sX2w"], 16 * NT)
                    for t in range(NT):
                        b = t % 2
                        if t >= 2:
                            s.wait_ge(S["vF23"], 2 * (t - 2) + 2)
                        s.dma_start(
                            uptile[:, :].rearrange("p (b c m) -> p b c m",
                                                   b=2, c=ND)[:, b],
                            bass.AP(upd, t * P,
                                    [[L, P], [P * L, ND], [1, P]]),
                        ).then_inc(S["sF3l"], 16)
                        s.dma_start(
                            h2tile[:, :].rearrange("p (b c m) -> p b c m",
                                                   b=2, c=ND)[:, b],
                            bass.AP(h2d, t * P,
                                    [[L, P], [P * L, ND], [1, P]]),
                        ).then_inc(S["sF3l"], 16)
                        s.dma_start(
                            x2tile[:, b * D:b * D + D],
                            bass.AP(x2d, t * P * D, [[D, P], [1, D]]),
                        ).then_inc(S["sF3l"], 16)

                @blk.tensor
                def _(tn):
                    tn.wait_ge(S["sWB0"], 256)
                    for t in range(NT):
                        b = t % 2
                        tn.wait_ge(S["sF3l"], 48 * (t + 1))
                        for half in range(2):
                            if t >= 1:
                                tn.wait_ge(S["vF23"], 2 * (t - 1) + half + 1)
                            for sl in range(2):
                                for o in range(ND):
                                    tn.matmul(
                                        sh_ps[half][:, sl * 512:sl * 512 + 512],
                                        uptile[:, b * D + o * P:b * D + o * P + P],
                                        WB1[:, o * D + half * 1024 + sl * 512:
                                            o * D + half * 1024 + sl * 512 + 512],
                                        start=(o == 0), stop=(o == ND - 1))
                            for sl in range(2):
                                for d in range(ND):
                                    mm = tn.matmul(
                                        ex_ps[half][:, sl * 512:sl * 512 + 512],
                                        h2tile[:, b * D + d * P:b * D + d * P + P],
                                        WB0[:, d * D + half * 1024 + sl * 512:
                                            d * D + half * 1024 + sl * 512 + 512],
                                        start=(d == 0), stop=(d == ND - 1))
                            mm.then_inc(S["pF2h"], 1)

                @blk.vector
                def _(v):
                    for t in range(NT):
                        b = t % 2
                        for half in range(2):
                            v.wait_ge(S["pF2h"], 2 * t + half + 1)
                            if half == 0 and t >= 2:
                                v.wait_ge(S["sOutW"], 16 * (t - 1))
                            v.scalar_tensor_tensor(
                                ftmp[:, :], ex_ps[half][:, :],
                                rw_all[:, t:t + 1],
                                x2tile[:, b * D + half * 1024:
                                       b * D + half * 1024 + 1024],
                                op0=ALU.mult, op1=ALU.add)
                            v.tensor_add(
                                outt[:, b * D + half * 1024:
                                     b * D + half * 1024 + 1024],
                                ftmp[:, :], sh_ps[half][:, :]
                            ).then_inc(S["vF23"], 1)

                @blk.scalar
                def _(sc):
                    for t in range(NT):
                        b = t % 2
                        sc.wait_ge(S["vF23"], 2 * t + 2)
                        sc.dma_start(
                            bass.AP(out, t * P * D, [[D, P], [1, D]]),
                            outt[:, b * D:b * D + D]).then_inc(S["sOutW"], 16)
                    sc.wait_ge(S["sOutW"], 16 * NT)

        if tap_outs:
            with nc.Block() as blk:
                @blk.sync
                def _(s):
                    for sem, tot in [("sH1w", 16 * NT), ("sQw", 16 * H * NTC),
                                     ("sKw", 16 * H * NTC),
                                     ("sVw", 16 * NTC * 16),
                                     ("sOw", 16 * H * NTC), ("sX2w", 16 * NT),
                                     ("sH2w", 16 * NT), ("sUw", 16 * NTC)]:
                        gate = {"sH1w": "ln1", "sQw": "q", "sKw": "k",
                                "sVw": "v", "sOw": "attn", "sX2w": "o",
                                "sH2w": "o", "sUw": "f1"}[sem]
                        if ph(gate):
                            s.wait_ge(S[sem], tot)
                    ntap = 0
                    for dst, src in tap_outs:
                        rows, cols = src.shape
                        s.dma_start(
                            bass.AP(dst, 0, [[cols, rows], [1, cols]]),
                            bass.AP(src, 0, [[cols, rows], [1, cols]]),
                        ).then_inc(S["sTap"], 16)
                        ntap += 1
                    s.wait_ge(S["sTap"], 16 * ntap)

    return nc


def host_inputs(L0_attn_x, L0_attn_W_qkv, L0_attn_W_o, L0_ffn_W_up,
                L0_ffn_W_down, L0_ffn_W_gate, L0_ffn_W_expert):
    bf16 = ml_dtypes.bfloat16
    x = np.ascontiguousarray(np.asarray(L0_attn_x, dtype=np.float32))
    wqkv = np.asarray(L0_attn_W_qkv, dtype=np.float32)
    pp = np.concatenate([np.arange(0, P, 2), np.arange(1, P, 2)])
    rows = np.arange(3 * D)
    for sec in (0, D):
        for h in range(H):
            base = sec + h * P
            rows[base:base + P] = base + pp
    wqkvT = np.ascontiguousarray(wqkv[rows].T)
    woT = np.ascontiguousarray(np.asarray(L0_attn_W_o, dtype=np.float32).T)
    wupT = np.ascontiguousarray(np.asarray(L0_ffn_W_up, dtype=np.float32).T)
    wdownT = np.ascontiguousarray(np.asarray(L0_ffn_W_down, dtype=np.float32).T)
    wexp = np.ascontiguousarray(np.asarray(L0_ffn_W_expert, dtype=np.float32))
    wgT = np.ascontiguousarray(np.asarray(L0_ffn_W_gate, dtype=np.float32).T)

    inv = (1.0 / (10000.0 ** (np.arange(0, P, 2, dtype=np.float32) / P))
           ).astype(np.float32)
    ang = np.arange(L, dtype=np.float32)[:, None] * inv[None, :]
    cosv = np.cos(ang).T.astype(np.float32)
    sinv = np.sin(ang).T.astype(np.float32)
    costab = np.ascontiguousarray(np.vstack([cosv, cosv]))
    nsintab = np.ascontiguousarray(np.vstack([-sinv, sinv]))

    identv = np.eye(P, dtype=bf16)
    masks = np.zeros((P, 4 * 512), dtype=np.float32)
    for j in range(4):
        masks[:, j * 512:(j + 1) * 512] = (
            (np.arange(512)[None, :] - P * j) >= np.arange(P)[:, None]
        ).astype(np.float32)
    masksv = masks.astype(bf16)

    shared = {"wqkvT": wqkvT, "woT": woT, "wupT": wupT, "wdownT": wdownT,
              "wexp": wexp, "wgT": wgT, "costab": costab, "nsintab": nsintab,
              "identin": identv, "masksin": masksv}
    in_maps = []
    for b in range(NCORES):
        m = dict(shared)
        m["x"] = np.ascontiguousarray(x[b])
        in_maps.append(m)
    return in_maps


def kernel(**inputs):
    in_maps = host_inputs(**inputs)
    nc = build_nc()
    res = run_bass_kernel_spmd(nc, in_maps, core_ids=list(range(NCORES)))
    return np.stack([np.asarray(r["out"], dtype=np.float32)
                     for r in res.results], axis=0)


# version 02
# speedup vs baseline: 1.0235x; 1.0235x over previous
import contextlib

import numpy as np
import ml_dtypes

import concourse.bass as bass
import concourse.mybir as mybir
from concourse.bass_utils import run_bass_kernel_spmd

F32 = mybir.dt.float32
BF16 = mybir.dt.bfloat16
AF = mybir.ActivationFunctionType
ALU = mybir.AluOpType
AX = mybir.AxisListType.X

P = 128
D = 2048
L = 4096
H = 16
ND = D // P
NT = L // P
NTC = L // 512
SCALE = float(1.0 / np.sqrt(128.0))
EPS = 1e-5
NCORES = 8


PH_ORDER = ["ln1", "q", "k", "v", "bias", "attn", "o", "f1", "f23"]


def build_nc(taps=(), upto="f23"):
    import os as _os
    _last = PH_ORDER.index(upto)
    _skip = set(x for x in _os.environ.get("SKIP_PH", "").split(",") if x)

    def ph(name):
        return PH_ORDER.index(name) <= _last and name not in _skip

    nc = bass.Bass("TRN2", target_bir_lowering=False)

    x_in = nc.dram_tensor("x", [L, D], F32, kind="ExternalInput")
    wqkvT = nc.dram_tensor("wqkvT", [D, 3 * D], F32, kind="ExternalInput")
    woT = nc.dram_tensor("woT", [D, D], F32, kind="ExternalInput")
    wupT = nc.dram_tensor("wupT", [D, D], F32, kind="ExternalInput")
    wdownT = nc.dram_tensor("wdownT", [D, D], F32, kind="ExternalInput")
    wexp = nc.dram_tensor("wexp", [D, D], F32, kind="ExternalInput")
    wgT = nc.dram_tensor("wgT", [D, 8], F32, kind="ExternalInput")
    costab = nc.dram_tensor("costab", [P, L], F32, kind="ExternalInput")
    nsintab = nc.dram_tensor("nsintab", [P, L], F32, kind="ExternalInput")
    identin = nc.dram_tensor("identin", [P, P], BF16, kind="ExternalInput")
    masksin = nc.dram_tensor("masksin", [P, 4 * 512], BF16, kind="ExternalInput")
    out = nc.dram_tensor("out", [L, D], F32, kind="ExternalOutput")

    h1d = nc.dram_tensor("h1d", [D, L], BF16)
    qd = nc.dram_tensor("qd", [D, L], BF16)
    kd = nc.dram_tensor("kd", [D, L], BF16)
    vt = nc.dram_tensor("vt", [L, D], BF16)
    otok = nc.dram_tensor("otok", [L, D], BF16)
    x2d = nc.dram_tensor("x2d", [L, D], F32)
    h2d = nc.dram_tensor("h2d", [D, L], BF16)
    upd = nc.dram_tensor("upd", [D, L], BF16)
    scr_map = {"h1d": h1d, "qd": qd, "kd": kd, "vt": vt, "otok": otok,
               "x2d": x2d, "h2d": h2d, "upd": upd}
    tap_outs = [(nc.dram_tensor("tap_" + n, list(scr_map[n].shape),
                                scr_map[n].dtype, kind="ExternalOutput"),
                 scr_map[n]) for n in taps]

    with contextlib.ExitStack() as ctx:
        e = ctx.enter_context

        ident = e(nc.sbuf_tensor("ident", [P, P], BF16))
        wg_sb = e(nc.sbuf_tensor("wg_sb", [P, ND * 8], BF16))
        ones_sb = e(nc.sbuf_tensor("ones_sb", [P, P], BF16))
        qn2 = e(nc.sbuf_tensor("qn2", [1, H], F32))
        kn2 = e(nc.sbuf_tensor("kn2", [1, H], F32))
        biasC = e(nc.sbuf_tensor("biasC", [P, H], F32))
        rw_all = e(nc.sbuf_tensor("rw_all", [P, NT], F32))
        smallA = e(nc.sbuf_tensor("smallA", [1, 16], F32))
        smallB = e(nc.sbuf_tensor("smallB", [1, 16], BF16))

        WB0 = e(nc.sbuf_tensor("WB0", [P, ND * D], BF16))
        WB1 = e(nc.sbuf_tensor("WB1", [P, ND * D], BF16))

        S = {}
        for name in [
            "sWB0", "sWB1", "sCst", "gOnes",
            "sX", "vH", "vvL", "aaL", "pTL", "vCpL", "sH1w",
            "sh1Q", "pMMQ", "vPQ", "vStQ", "vSqQ", "pQNQ", "vQNQ", "sQw", "vvQ", "pTCQ",
            "sh1K", "pMMK", "vPK", "vStK", "vSqK", "pQNK", "vQNK", "sKw", "vvK", "pTCK",
            "sh1V", "pMMV", "aVV", "sVw",
            "vvB", "aaB", "pB",
            "sKV", "sQl", "pS", "aExp", "vMask", "pPt", "pOd", "vOev", "sOw",
            "pQd", "pHd", "vvA", "sMsk",
            "sOl", "pT1", "aCp1", "pWo", "vEvO", "pLg", "vGt", "sX2w",
            "sH2w", "vvO", "aaO", "vH2",
            "sh2F", "pF1", "aRl", "sUw", "pTCF",
            "sF3l", "pF2h", "vF23", "sOutW",
            "sTap",
        ]:
            S[name] = e(nc.semaphore(name))

        def wload(g, dst, src, src_ld, col0, ncols, sem):
            for part in range(4):
                dst_ap = dst[:, :].rearrange("p (c m) -> p c m", c=ND)[
                    :, 4 * part:4 * part + 4, 0:ncols]
                src_ap = bass.AP(
                    src, (4 * part) * P * src_ld + col0,
                    [[src_ld, P], [P * src_ld, 4], [1, ncols]])
                g.dma_start(dst_ap, src_ap).then_inc(sem, 16)

        with contextlib.ExitStack() as lc:
            el = lc.enter_context
            Xt = el(nc.sbuf_tensor("XtL", [P, 2 * D], F32))
            scr = el(nc.sbuf_tensor("scrL", [P, D], F32))
            hbf = el(nc.sbuf_tensor("hbfL", [P, 2 * D], BF16))
            h1st = el(nc.sbuf_tensor("h1stL", [P, 2 * D], BF16))
            lnst = el(nc.sbuf_tensor("lnstL", [P, 16], F32))
            tp = [el(nc.psum_tensor("tp%d" % i_, [P, P], BF16))
                  for i_ in range(4)]

            with nc.Block() as blk:
                @blk.gpsimd
                def _(g):
                    g.memset(ones_sb[:, :], 1.0)
                    g.memset(qn2[:, :], 0.0)
                    g.memset(kn2[:, :], 0.0).then_inc(S["gOnes"], 1)
                    wload(g, WB0, wqkvT, 3 * D, 0, D, S["sWB0"])
                    wload(g, WB1, wqkvT, 3 * D, D, D, S["sWB1"])
                    g.dma_start(
                        wg_sb[:, :].rearrange("p (c m) -> p c m", c=ND),
                        bass.AP(wgT, 0, [[8, P], [P * 8, ND], [1, 8]]),
                    ).then_inc(S["sCst"], 16)
                    for t in range(NT):
                        b = t % 2
                        g.wait_ge(S["vCpL"], ND * (t + 1))
                        g.dma_start(
                            bass.AP(h1d, t * P, [[L, P], [P * L, ND], [1, P]]),
                            h1st[:, :].rearrange("p (b c m) -> p b c m",
                                                 b=2, c=ND)[:, b],
                        ).then_inc(S["sH1w"], 16)

                @blk.sync
                def _(s):
                    s.dma_start(ident[:, :], identin[:, :]).then_inc(S["sCst"], 16)
                    for t in range(NT):
                        if t >= 2:
                            s.wait_ge(S["vH"], t - 1)
                        s.dma_start(
                            Xt[:, (t % 2) * D:(t % 2) * D + D],
                            bass.AP(x_in, t * P * D, [[D, P], [1, D]]),
                        ).then_inc(S["sX"], 16)

                def emit_ln_copies(v, t):
                    b = t % 2
                    if t >= 2:
                        v.wait_ge(S["sH1w"], 16 * (t - 1))
                    for c in range(ND):
                        i = t * ND + c
                        v.wait_ge(S["pTL"], i + 1)
                        v.tensor_copy(h1st[:, b * D + c * P:b * D + c * P + P],
                                      tp[i % 4][:, :]).then_inc(S["vCpL"], 1)

                @blk.vector
                def _(v):
                    n = 0
                    for t in range(NT):
                        b = t % 2
                        xa = Xt[:, b * D:b * D + D]
                        v.wait_ge(S["sX"], 16 * (t + 1))
                        if t >= 2:
                            v.wait_ge(S["pTL"], ND * (t - 1))
                        v.scalar_tensor_tensor(
                            scr[:, :], xa, 1.0, xa, op0=ALU.mult, op1=ALU.mult,
                            accum_out=lnst[:, 1:2]).then_inc(S["vvL"], 1)
                        v.reduce_sum(lnst[:, 0:1], xa, axis=AX
                                     ).then_inc(S["vvL"], 1)
                        n += 2
                        v.wait_ge(S["vvL"], n)
                        v.tensor_scalar_mul(lnst[:, 2:3], lnst[:, 0:1], 1.0 / D
                                            ).then_inc(S["vvL"], 1)
                        v.tensor_scalar(lnst[:, 3:4], lnst[:, 1:2], 1.0 / D,
                                        EPS, op0=ALU.mult, op1=ALU.add
                                        ).then_inc(S["vvL"], 1)
                        n += 2
                        v.wait_ge(S["vvL"], n)
                        v.tensor_mul(lnst[:, 4:5], lnst[:, 2:3], lnst[:, 2:3]
                                     ).then_inc(S["vvL"], 1)
                        n += 1
                        v.wait_ge(S["vvL"], n)
                        v.tensor_sub(lnst[:, 5:6], lnst[:, 3:4], lnst[:, 4:5]
                                     ).then_inc(S["vvL"], 1)
                        n += 1
                        v.wait_ge(S["aaL"], 2 * (t + 1))
                        v.tensor_scalar(lnst[:, 8:9], lnst[:, 2:3],
                                        lnst[:, 7:8], -1.0,
                                        op0=ALU.mult, op1=ALU.mult
                                        ).then_inc(S["vvL"], 1)
                        n += 1
                        v.wait_ge(S["vvL"], n)
                        v.tensor_scalar(
                            hbf[:, b * D:b * D + D], xa,
                            lnst[:, 7:8], lnst[:, 8:9],
                            op0=ALU.mult, op1=ALU.add).then_inc(S["vH"], 1)
                        if t >= 1:
                            emit_ln_copies(v, t - 1)
                    emit_ln_copies(v, NT - 1)

                @blk.scalar
                def _(sc):
                    for t in range(NT):
                        sc.wait_ge(S["vvL"], 7 * t + 6)
                        sc.activation(lnst[:, 6:7], lnst[:, 5:6], AF.Ln
                                      ).then_inc(S["aaL"], 1)
                        sc.wait_ge(S["aaL"], 2 * t + 1)
                        sc.activation(lnst[:, 7:8], lnst[:, 6:7], AF.Exp,
                                      scale=-0.5).then_inc(S["aaL"], 1)

                @blk.tensor
                def _(tn):
                    tn.wait_ge(S["sCst"], 32)
                    for t in range(NT):
                        b = t % 2
                        tn.wait_ge(S["vH"], t + 1)
                        for c in range(ND):
                            i = t * ND + c
                            if i >= 4:
                                tn.wait_ge(S["vCpL"], i - 3)
                            tn.transpose(
                                tp[i % 4][:, :],
                                hbf[:, b * D + c * P:b * D + c * P + P],
                                ident[:, :]).then_inc(S["pTL"], 1)

        def qk_phase(tag, col0, WB, sWBn, wbtgt, sh1, pMM, vP, vSt, vSq, pQN,
                     vQN, sW, pTC, stats, dst, prefetch):
            with contextlib.ExitStack() as qc_:
                eq = qc_.enter_context
                h1c = eq(nc.sbuf_tensor("h1c" + tag, [P, 2 * ND * 512], BF16))
                cosS = eq(nc.sbuf_tensor("cosS" + tag, [P, 2 * 512], F32))
                nsinS = eq(nc.sbuf_tensor("nsinS" + tag, [P, 2 * 512], F32))
                ropeA = eq(nc.sbuf_tensor("ropeA" + tag, [P, 512], F32))
                ropeB = eq(nc.sbuf_tensor("ropeB" + tag, [P, 512], F32))
                qst = eq(nc.sbuf_tensor("qst" + tag, [P, 2 * 512], BF16))
                sqb = eq(nc.sbuf_tensor("sqb" + tag, [P, 2 * 512], BF16))
                qp = [eq(nc.psum_tensor("qpA" + tag, [P, 512], F32)),
                      eq(nc.psum_tensor("qpB" + tag, [P, 512], F32))]
                qn = [eq(nc.psum_tensor("qnA" + tag, [1, 512], F32)),
                      eq(nc.psum_tensor("qnB" + tag, [1, 512], F32))]
                NI = NTC * H

                with nc.Block() as blk:
                    @blk.sync
                    def _(s):
                        for tc in range(NTC):
                            if tc >= 2:
                                s.wait_ge(pMM, H * (tc - 1))
                            s.wait_ge(S["sH1w"], 16 * 4 * (tc + 1))
                            b = tc % 2
                            s.dma_start(
                                h1c[:, :].rearrange("p (b c m) -> p b c m",
                                                    b=2, c=ND)[:, b],
                                bass.AP(h1d, tc * 512,
                                        [[L, P], [P * L, ND], [1, 512]]),
                            ).then_inc(sh1, 16)
                            s.dma_start(cosS[:, b * 512:b * 512 + 512],
                                        costab[:, tc * 512:tc * 512 + 512]
                                        ).then_inc(sh1, 16)
                            s.dma_start(nsinS[:, b * 512:b * 512 + 512],
                                        nsintab[:, tc * 512:tc * 512 + 512]
                                        ).then_inc(sh1, 16)

                    def norm_mm(tn, i):
                        tn.wait_ge(vSq, i + 1)
                        if i >= 2:
                            tn.wait_ge(vQN, i - 1)
                        tn.matmul(qn[i % 2][:, :], ones_sb[:, 0:1],
                                  sqb[:, (i % 2) * 512:(i % 2) * 512 + 512],
                                  start=True, stop=True).then_inc(pQN, 1)

                    @blk.tensor
                    def _(tn):
                        tn.wait_ge(sWBn, wbtgt)
                        tn.wait_ge(S["gOnes"], 1)
                        for tc in range(NTC):
                            tn.wait_ge(sh1, 48 * (tc + 1))
                            b = tc % 2
                            for ee in range(H):
                                i = tc * H + ee
                                if i >= 2:
                                    tn.wait_ge(vP, i - 1)
                                for d in range(ND):
                                    mm = tn.matmul(
                                        qp[i % 2][:, :],
                                        WB[:, d * D + ee * P:d * D + ee * P + P],
                                        h1c[:, b * ND * 512 + d * 512:
                                            b * ND * 512 + d * 512 + 512],
                                        start=(d == 0), stop=(d == ND - 1))
                                mm.then_inc(pMM, 1)
                                if i >= 1:
                                    norm_mm(tn, i - 1)
                        norm_mm(tn, NI - 1)

                    def qk_norm_red(v, j, nv):
                        eej = j % H
                        v.wait_ge(pQN, j + 1)
                        v.reduce_max(smallA[0:1, 0:1], qn[j % 2][0:1, :],
                                     axis=AX).then_inc(S["vv" + tag], 1)
                        nv += 1
                        v.wait_ge(S["vv" + tag], nv)
                        v.tensor_max(stats[0:1, eej:eej + 1],
                                     stats[0:1, eej:eej + 1],
                                     smallA[0:1, 0:1]).then_inc(vQN, 1)
                        return nv

                    @blk.vector
                    def _(v):
                        nv = 0
                        for tc in range(NTC):
                            b = tc % 2
                            for ee in range(H):
                                i = tc * H + ee
                                sp = i % 2
                                v.wait_ge(pMM, i + 1)
                                cs = cosS[:, b * 512:b * 512 + 512]
                                ns = nsinS[:, b * 512:b * 512 + 512]
                                v.tensor_mul(ropeA[:, :], qp[sp][:, :], cs)
                                v.tensor_mul(ropeB[0:64, :],
                                             qp[sp][64:128, :], ns[0:64, :])
                                v.tensor_mul(ropeB[64:128, :],
                                             qp[sp][0:64, :], ns[64:128, :]
                                             ).then_inc(vP, 1)
                                if i >= 2:
                                    v.wait_ge(sW, 16 * (i - 1))
                                    v.wait_ge(vSq, i - 1)
                                v.tensor_add(qst[:, sp * 512:sp * 512 + 512],
                                             ropeA[:, :], ropeB[:, :]
                                             ).then_inc(vSt, 1)
                                if i >= 1:
                                    nv = qk_norm_red(v, i - 1, nv)
                        qk_norm_red(v, NI - 1, nv)

                    @blk.scalar
                    def _(sc):
                        for tc in range(NTC):
                            for ee in range(H):
                                i = tc * H + ee
                                sp = i % 2
                                sc.wait_ge(vSt, i + 1)
                                sc.dma_start(
                                    bass.AP(dst, (ee * P) * L + tc * 512,
                                            [[L, P], [1, 512]]),
                                    qst[:, sp * 512:sp * 512 + 512],
                                ).then_inc(sW, 16)

                    @blk.gpsimd
                    def _(g):
                        if prefetch is not None:
                            pf_dst, pf_src, pf_ld, pf_col, pf_sem = prefetch
                            wload(g, pf_dst, pf_src, pf_ld, pf_col, D, pf_sem)
                        for i in range(NI):
                            sp = i % 2
                            g.wait_ge(vSt, i + 1)
                            if i >= 2:
                                g.wait_ge(pQN, i - 1)
                            g.tensor_mul(sqb[:, sp * 512:sp * 512 + 512],
                                         qst[:, sp * 512:sp * 512 + 512],
                                         qst[:, sp * 512:sp * 512 + 512]
                                         ).then_inc(vSq, 1)

        if ph("q"):
            qk_phase("Q", 0, WB0, S["sWB0"], 64, S["sh1Q"], S["pMMQ"], S["vPQ"],
                     S["vStQ"], S["vSqQ"], S["pQNQ"], S["vQNQ"], S["sQw"],
                     S["pTCQ"], qn2, qd, None)
        if ph("k"):
            qk_phase("K", D, WB1, S["sWB1"], 64, S["sh1K"], S["pMMK"], S["vPK"],
                     S["vStK"], S["vSqK"], S["pQNK"], S["vQNK"], S["sKw"],
                     S["pTCK"], kn2, kd,
                     (WB0, wqkvT, 3 * D, 2 * D, S["sWB0"]))

        if ph("v"):
         with contextlib.ExitStack() as vc_:
            ev2 = vc_.enter_context
            h1cV = ev2(nc.sbuf_tensor("h1cV", [P, 2 * ND * 512], BF16))
            vst = ev2(nc.sbuf_tensor("vst", [P, 2 * 512], BF16))
            vp = [ev2(nc.psum_tensor("vpA", [P, 512], F32)),
                  ev2(nc.psum_tensor("vpB", [P, 512], F32))]

            with nc.Block() as blk:
                @blk.sync
                def _(s):
                    for tc in range(NTC):
                        if tc >= 2:
                            s.wait_ge(S["pMMV"], 16 * (tc - 1))
                        b = tc % 2
                        s.dma_start(
                            h1cV[:, :].rearrange("p (b c m) -> p b c m",
                                                 b=2, c=ND)[:, b],
                            bass.AP(h1d, tc * 512,
                                    [[L, P], [P * L, ND], [1, 512]]),
                        ).then_inc(S["sh1V"], 16)

                @blk.tensor
                def _(tn):
                    tn.wait_ge(S["sWB0"], 128)
                    for tc in range(NTC):
                        tn.wait_ge(S["sh1V"], 16 * (tc + 1))
                        b = tc % 2
                        for t4 in range(4):
                            for e4 in range(4):
                                i = (tc * 4 + t4) * 4 + e4
                                if i >= 2:
                                    tn.wait_ge(S["aVV"], i - 1)
                                for d in range(ND):
                                    mm = tn.matmul(
                                        vp[i % 2][:, :],
                                        h1cV[:, b * ND * 512 + d * 512 + t4 * P:
                                             b * ND * 512 + d * 512 + t4 * P + P],
                                        WB0[:, d * D + e4 * 512:
                                            d * D + e4 * 512 + 512],
                                        start=(d == 0), stop=(d == ND - 1))
                                mm.then_inc(S["pMMV"], 1)

                @blk.scalar
                def _(sc):
                    for i in range(NTC * 16):
                        tc, rem = divmod(i, 16)
                        t4, e4 = divmod(rem, 4)
                        sp = i % 2
                        sc.wait_ge(S["pMMV"], i + 1)
                        if i >= 2:
                            sc.wait_ge(S["sVw"], 16 * (i - 1))
                        sc.copy(vst[:, sp * 512:sp * 512 + 512], vp[sp][:, :]
                                ).then_inc(S["aVV"], 1)
                        sc.wait_ge(S["aVV"], i + 1)
                        sc.dma_start(
                            bass.AP(vt, (tc * 512 + t4 * P) * D + e4 * 512,
                                    [[D, P], [1, 512]]),
                            vst[:, sp * 512:sp * 512 + 512],
                        ).then_inc(S["sVw"], 16)

                @blk.gpsimd
                def _(g):
                    wload(g, WB1, woT, D, 0, D, S["sWB1"])

        if ph("bias"):
         with contextlib.ExitStack() as bc_:
            bps = bc_.enter_context(nc.psum_tensor("bps", [P, H], F32))
            with nc.Block() as blk:
                @blk.vector
                def _(v):
                    v.tensor_mul(smallA[0:1, 0:16], qn2[0:1, :], kn2[0:1, :]
                                 ).then_inc(S["vvB"], 1)
                    v.wait_ge(S["vvB"], 1)
                    v.tensor_copy(smallB[0:1, 0:16], smallA[0:1, 0:16]
                                  ).then_inc(S["vvB"], 1)
                    v.wait_ge(S["aaB"], 2)
                    v.tensor_scalar(biasC[:, :], biasC[:, :], -SCALE, -1.0,
                                    op0=ALU.mult, op1=ALU.add
                                    ).then_inc(S["vvB"], 1)

                @blk.tensor
                def _(tn):
                    tn.wait_ge(S["vvB"], 2)
                    for h in range(H):
                        mm = tn.matmul(bps[:, h:h + 1], ones_sb[0:1, :],
                                       smallB[0:1, h:h + 1], start=True,
                                       stop=True)
                    mm.then_inc(S["pB"], 1)

                @blk.scalar
                def _(sc):
                    sc.wait_ge(S["pB"], 1)
                    sc.activation(biasC[:, :], bps[:, :], AF.Ln
                                  ).then_inc(S["aaB"], 1)
                    sc.wait_ge(S["aaB"], 1)
                    sc.activation(biasC[:, :], biasC[:, :], AF.Exp, scale=0.5
                                  ).then_inc(S["aaB"], 1)

        import os
        _astage = int(os.environ.get("ATT_STAGE", "4"))
        _njq = int(os.environ.get("ATT_J", str(H * NTC)))
        ahq = [(h, qc) for h in range(H) for qc in range(NTC)][:_njq]
        apairs = []
        pair_of = {}
        diag_ord = {}
        for j, (h, qc) in enumerate(ahq):
            for pp in range(2 * qc + 2):
                pair_of[(j, pp)] = len(apairs)
                apairs.append((j, h, qc, pp, pp >= 2 * qc))
        ndi = 0
        for g, (j, h, qc, pp, diag) in enumerate(apairs):
            if diag:
                ndi += 1
                diag_ord[g] = ndi
        cum_pairs = [0] * len(ahq)
        _acc = 0
        for j, (h, qc) in enumerate(ahq):
            _acc += 2 * qc + 2
            cum_pairs[j] = _acc

        VW = 132
        if ph("attn"):
         with contextlib.ExitStack() as ac_:
            ea = ac_.enter_context
            Kb = ea(nc.sbuf_tensor("Kb", [P, 2 * L], BF16))
            Vb = ea(nc.sbuf_tensor("Vb", [P, 2 * 32 * VW], BF16))
            Qb = ea(nc.sbuf_tensor("Qb", [P, 2 * 512], BF16))
            Pt = ea(nc.sbuf_tensor("Pt", [P, 2 * 1024], BF16))
            masks_sb = ea(nc.sbuf_tensor("masks_sb", [P, 4 * 512], BF16))
            biasX = ea(nc.sbuf_tensor("biasX", [P, H], F32))
            Otst = ea(nc.sbuf_tensor("Otst", [P, 2 * 512], BF16))
            rcA = ea(nc.sbuf_tensor("rcA", [P, 8], F32))
            OtF = ea(nc.sbuf_tensor("OtF", [P, 4 * 132], F32))
            Sps = [ea(nc.psum_tensor("SpsA", [P, 1024], F32)),
                   ea(nc.psum_tensor("SpsB", [P, 1024], F32))]
            Ops = [ea(nc.psum_tensor("Ops%d" % qs, [P, 512], F32))
                   for qs in range(4)]

            with nc.Block() as blk:
                @blk.gpsimd
                def _(g):
                    g.memset(Vb[:, :].rearrange("p (n c) -> p n c", c=VW)
                             [:, :, 128:129], 1.0).then_inc(S["gOnes"], 1)
                    wload(g, WB0, wupT, D, 0, D, S["sWB0"])
                    for j, (h, qc) in enumerate(ahq):
                        if _astage < 4 or float(os.environ.get("ATT_EVAC", "3")) < 3:
                            break
                        g.wait_ge(S["vOev"], j + 1)
                        g.dma_start(
                            bass.AP(otok, (qc * 512) * D + h * P,
                                    [[D, P], [P * D, 4], [1, P]]),
                            Otst[:, :].rearrange("p (b q m) -> p b q m",
                                                 b=2, q=4)[:, j % 2],
                        ).then_inc(S["sOw"], 16)

                _nheads = (len(ahq) + NTC - 1) // NTC
                @blk.sync
                def _(s):
                    s.dma_start(masks_sb[:, :], masksin[:, :]
                                ).then_inc(S["sMsk"], 16)
                    if ph("k"):
                        s.wait_ge(S["sKw"], 16 * H * NTC)
                    if ph("v"):
                        s.wait_ge(S["sVw"], 16 * NTC * 16)
                    if ph("q"):
                        s.wait_ge(S["sQw"], 16 * H * NTC)
                    for h in range(_nheads):
                        hb = h % 2
                        if h >= 2:
                            s.wait_ge(S["pPt"], cum_pairs[(h - 1) * NTC - 1])
                        s.dma_start(
                            Kb[:, hb * L:hb * L + L],
                            bass.AP(kd, (h * P) * L, [[L, P], [1, L]]),
                        ).then_inc(S["sKV"], 16)
                        s.dma_start(
                            Vb[:, :].rearrange("p (b n c) -> p b n c",
                                               b=2, n=32)[:, hb, :, 0:128],
                            bass.AP(vt, h * P, [[D, P], [P * D, 32], [1, P]]),
                        ).then_inc(S["sKV"], 16)
                        for qc in range(NTC):
                            j = h * NTC + qc
                            if j >= len(ahq):
                                break
                            if j >= 2:
                                s.wait_ge(S["pS"], cum_pairs[j - 2])
                            s.dma_start(
                                Qb[:, (j % 2) * 512:(j % 2) * 512 + 512],
                                bass.AP(qd, (h * P) * L + qc * 512,
                                        [[L, P], [1, 512]]),
                            ).then_inc(S["sQl"], 16)

                def emit_pv(tn, j, h, qc, pp):
                    if _astage < 3:
                        return
                    g = pair_of[(j, pp)]
                    sp = g % 2
                    hb = h % 2
                    npair = 2 * qc + 2
                    last = pp == npair - 1
                    if _astage >= 2 and pp >= 2 * qc:
                        tn.wait_ge(S["vMask"], diag_ord[g])
                    else:
                        tn.wait_ge(S["aExp"], g + 1)
                    if _astage >= 4 and pp == 0 and j >= 1:
                        tn.wait_ge(S["vOev"], j)
                    for half in range(2):
                        kb = 2 * pp + half
                        for qs in range(4):
                            mm = tn.matmul(
                                Ops[qs][:, 0:129],
                                Pt[:, sp * 1024 + half * 512 + qs * P:
                                   sp * 1024 + half * 512 + qs * P + P],
                                Vb[:, hb * 32 * VW + kb * VW:
                                   hb * 32 * VW + kb * VW + 129],
                                start=(kb == 0), stop=(kb == 2 * npair - 1))
                    mm.then_inc(S["pPt"], 1)

                @blk.tensor
                def _(tn):
                    tn.wait_ge(S["gOnes"], (2 if ph("ln1") else 1))
                    for j, (h, qc) in enumerate(ahq):
                        hb = h % 2
                        tn.wait_ge(S["sKV"], 32 * (h + 1))
                        tn.wait_ge(S["sQl"], 16 * (j + 1))
                        npair = 2 * qc + 2
                        for pp in range(npair):
                            g = pair_of[(j, pp)]
                            sp = g % 2
                            if _astage >= 1 and g >= 2:
                                tn.wait_ge(S["aExp"], g - 1)
                            for half in range(2):
                                kb = 2 * pp + half
                                mm = tn.matmul(
                                    Sps[sp][:, half * 512:half * 512 + 512],
                                    Kb[:, hb * L + kb * P:hb * L + kb * P + P],
                                    Qb[:, (j % 2) * 512:(j % 2) * 512 + 512],
                                    start=True, stop=True)
                            mm.then_inc(S["pS"], 1)
                            if pp >= 1:
                                emit_pv(tn, j, h, qc, pp - 1)
                        emit_pv(tn, j, h, qc, npair - 1)

                @blk.scalar
                def _(sc):
                    for g, (j, h, qc, pp, diag) in enumerate(apairs):
                        if _astage < 1:
                            break
                        sp = g % 2
                        sc.wait_ge(S["pS"], g + 1)
                        if _astage >= 3 and g >= 2:
                            sc.wait_ge(S["pPt"], g - 1)
                        _bias = (0.0 if os.environ.get("ATT_NOBIAS")
                                 else biasX[:, h:h + 1])
                        if g == 0:
                            sc.wait_ge(S["vvA"], 1)
                        sc.activation(Pt[:, sp * 1024:sp * 1024 + 1024],
                                      Sps[sp][:, :], AF.Exp,
                                      bias=_bias, scale=SCALE
                                      ).then_inc(S["aExp"], 1)

                @blk.vector
                def _(v):
                    v.tensor_copy(biasX[:, :], biasC[:, :]).then_inc(S["vvA"], 1)
                    v.wait_ge(S["sMsk"], 16)
                    nva = 1
                    for j, (h, qc) in enumerate(ahq):
                        if _astage >= 2:
                            for pp in (2 * qc, 2 * qc + 1):
                                g = pair_of[(j, pp)]
                                sp = g % 2
                                v.wait_ge(S["aExp"], g + 1)
                                for half in range(2):
                                    jd = 2 * (pp - 2 * qc) + half
                                    mul = v.tensor_mul(
                                        Pt[:, sp * 1024 + half * 512:
                                           sp * 1024 + half * 512 + 512],
                                        Pt[:, sp * 1024 + half * 512:
                                           sp * 1024 + half * 512 + 512],
                                        masks_sb[:, jd * 512:jd * 512 + 512])
                                mul.then_inc(S["vMask"], 1)
                        if _astage < 4:
                            continue
                        _evac = float(os.environ.get("ATT_EVAC", "3"))
                        v.wait_ge(S["pPt"], cum_pairs[j])
                        if _evac >= 3 and j >= 2:
                            v.wait_ge(S["sOw"], 16 * (j - 1))
                        if _evac >= 1.1:
                            for qs in range(4):
                                v.tensor_copy(OtF[:, qs * 132:qs * 132 + 129],
                                              Ops[qs][:, 0:129]
                                              ).then_inc(S["vvA"], 1)
                                nva += 1
                            v.wait_ge(S["vvA"], nva)
                        if _evac >= 1.2:
                            for qs in range(4):
                                v.reciprocal(rcA[:, qs:qs + 1],
                                             OtF[:, qs * 132 + 128:qs * 132 + 129]
                                             ).then_inc(S["vvA"], 1)
                                nva += 1
                            v.wait_ge(S["vvA"], nva)
                        if _evac >= 2:
                            for qs in range(4):
                                ts = v.tensor_scalar_mul(
                                    Otst[:, (j % 2) * 512 + qs * P:
                                         (j % 2) * 512 + qs * P + P],
                                    OtF[:, qs * 132:qs * 132 + P],
                                    rcA[:, qs:qs + 1])
                            ts.then_inc(S["vOev"], 1)
                        else:
                            v.nop().then_inc(S["vOev"], 1)

        if ph("o"):
         with contextlib.ExitStack() as oc_:
            eo = oc_.enter_context
            Otile = eo(nc.sbuf_tensor("Otile", [P, 2 * D], BF16))
            XtO = eo(nc.sbuf_tensor("XtO", [P, 2 * D], F32))
            x2t = eo(nc.sbuf_tensor("x2t", [P, 2 * D], F32))
            h2b = eo(nc.sbuf_tensor("h2b", [P, 2 * D], BF16))
            OfmT = eo(nc.sbuf_tensor("OfmT", [P, 2 * D], BF16))
            h2T = eo(nc.sbuf_tensor("h2T", [P, 2 * D], BF16))
            lnstO = eo(nc.sbuf_tensor("lnstO", [P, 16], F32))
            lgsb = eo(nc.sbuf_tensor("lgsb", [P, 8], F32))
            gtmp = eo(nc.sbuf_tensor("gtmp", [P, 24], F32))
            tpO = [eo(nc.psum_tensor("tpO%d" % i_, [P, P], BF16))
                   for i_ in range(3)]
            attn_ps = [eo(nc.psum_tensor("attnA", [P, 1024], F32)),
                       eo(nc.psum_tensor("attnB", [P, 1024], F32))]
            lgps = eo(nc.psum_tensor("lgps", [P, 8], F32))

            vvo_ln = []
            vvo_gate = []

            with nc.Block() as blk:
                @blk.sync
                def _(s):
                    s.wait_ge(S["sOw"], 16 * H * NTC)
                    for t in range(NT):
                        b = t % 2
                        if t >= 2:
                            s.wait_ge(S["pT1"], 32 * (t - 2) + 16)
                            s.wait_ge(S["vEvO"], 2 * (t - 2) + 2)
                        s.dma_start(
                            Otile[:, b * D:b * D + D],
                            bass.AP(otok, t * P * D, [[D, P], [1, D]]),
                        ).then_inc(S["sOl"], 16)
                        s.dma_start(
                            XtO[:, b * D:b * D + D],
                            bass.AP(x_in, t * P * D, [[D, P], [1, D]]),
                        ).then_inc(S["sOl"], 16)

                @blk.tensor
                def _(tn):
                    tn.wait_ge(S["sWB1"], 128)
                    for t in range(NT):
                        b = t % 2
                        tn.wait_ge(S["sOl"], 32 * (t + 1))
                        for c in range(ND):
                            i = t * 32 + c
                            if i >= 3:
                                tn.wait_ge(S["aCp1"], i - 2)
                            tn.transpose(tpO[i % 3][:, :],
                                         Otile[:, b * D + c * P:b * D + c * P + P],
                                         ident[:, :]).then_inc(S["pT1"], 1)
                        for half in range(2):
                            if half == 0:
                                tn.wait_ge(S["aCp1"], t * 32 + ND)
                            if t >= 1:
                                tn.wait_ge(S["vEvO"], 2 * (t - 1) + half + 1)
                            for d in range(ND):
                                for sl in range(2):
                                    mm = tn.matmul(
                                        attn_ps[half][:, sl * 512:sl * 512 + 512],
                                        OfmT[:, b * D + d * P:b * D + d * P + P],
                                        WB1[:, d * D + half * 1024 + sl * 512:
                                            d * D + half * 1024 + sl * 512 + 512],
                                        start=(d == 0), stop=(d == ND - 1))
                            mm.then_inc(S["pWo"], 1)
                        tn.wait_ge(S["vH2"], t + 1)
                        for c in range(ND):
                            i = t * 32 + 16 + c
                            tn.wait_ge(S["aCp1"], i - 2)
                            tn.transpose(tpO[i % 3][:, :],
                                         h2b[:, b * D + c * P:b * D + c * P + P],
                                         ident[:, :]).then_inc(S["pT1"], 1)
                        tn.wait_ge(S["aCp1"], 32 * (t + 1))
                        if t >= 1:
                            tn.wait_ge(S["vGt"], t)
                        for d in range(ND):
                            mm = tn.matmul(lgps[:, :],
                                           h2T[:, b * D + d * P:b * D + d * P + P],
                                           wg_sb[:, d * 8:d * 8 + 8],
                                           start=(d == 0), stop=(d == ND - 1))
                        mm.then_inc(S["pLg"], 1)

                @blk.vector
                def _(v):
                    nvo = 0
                    for t in range(NT):
                        b = t % 2
                        for half in range(2):
                            v.wait_ge(S["pWo"], 2 * t + half + 1)
                            if half == 0 and t >= 2:
                                v.wait_ge(S["sX2w"], 16 * (t - 1))
                            add = v.tensor_add(
                                x2t[:, b * D + half * 1024:b * D + half * 1024 + 1024],
                                XtO[:, b * D + half * 1024:b * D + half * 1024 + 1024],
                                attn_ps[half][:, :])
                            if half == 0:
                                add.then_inc(S["vEvO"], 1)
                        x2a = x2t[:, b * D:b * D + D]
                        v.scalar_tensor_tensor(
                            OfmT[:, b * D:b * D + D], x2a, 1.0, x2a,
                            op0=ALU.mult, op1=ALU.mult,
                            accum_out=lnstO[:, 1:2]).then_inc(S["vvO"], 1)
                        v.reduce_sum(lnstO[:, 0:1], x2a, axis=AX
                                     ).then_inc(S["vvO"], 1)
                        nvo += 2
                        v.wait_ge(S["vvO"], nvo)
                        v.nop().then_inc(S["vEvO"], 1)
                        v.tensor_scalar_mul(lnstO[:, 2:3], lnstO[:, 0:1], 1.0 / D
                                            ).then_inc(S["vvO"], 1)
                        v.tensor_scalar(lnstO[:, 3:4], lnstO[:, 1:2], 1.0 / D,
                                        EPS, op0=ALU.mult, op1=ALU.add
                                        ).then_inc(S["vvO"], 1)
                        nvo += 2
                        v.wait_ge(S["vvO"], nvo)
                        v.tensor_mul(lnstO[:, 4:5], lnstO[:, 2:3],
                                     lnstO[:, 2:3]).then_inc(S["vvO"], 1)
                        nvo += 1
                        v.wait_ge(S["vvO"], nvo)
                        v.tensor_sub(lnstO[:, 5:6], lnstO[:, 3:4],
                                     lnstO[:, 4:5]).then_inc(S["vvO"], 1)
                        nvo += 1
                        vvo_ln.append(nvo)
                        v.wait_ge(S["aaO"], 3 * t + 2)
                        v.tensor_scalar(lnstO[:, 8:9], lnstO[:, 2:3],
                                        lnstO[:, 7:8], -1.0,
                                        op0=ALU.mult, op1=ALU.mult
                                        ).then_inc(S["vvO"], 1)
                        nvo += 1
                        v.wait_ge(S["vvO"], nvo)
                        if t >= 2:
                            v.wait_ge(S["pT1"], 32 * (t - 1))
                        v.tensor_scalar(h2b[:, b * D:b * D + D], x2a,
                                        lnstO[:, 7:8], lnstO[:, 8:9],
                                        op0=ALU.mult, op1=ALU.add
                                        ).then_inc(S["vH2"], 1)
                        v.wait_ge(S["pLg"], t + 1)
                        v.tensor_copy(lgsb[:, :], lgps[:, :]).then_inc(S["vvO"], 1)
                        nvo += 1
                        v.wait_ge(S["vvO"], nvo)
                        v.nop().then_inc(S["vGt"], 1)
                        v.reduce_max(gtmp[:, 0:1], lgsb[:, :], axis=AX
                                     ).then_inc(S["vvO"], 1)
                        nvo += 1
                        v.wait_ge(S["vvO"], nvo)
                        v.tensor_scalar(gtmp[:, 8:16], lgsb[:, :],
                                        gtmp[:, 0:1], None, op0=ALU.subtract
                                        ).then_inc(S["vvO"], 1)
                        nvo += 1
                        v.wait_ge(S["vvO"], nvo)
                        v.tensor_scalar(gtmp[:, 16:24], gtmp[:, 8:16], 0.0,
                                        None, op0=ALU.is_ge
                                        ).then_inc(S["vvO"], 1)
                        nvo += 1
                        v.wait_ge(S["vvO"], nvo)
                        v.scalar_tensor_tensor(gtmp[:, 8:16], gtmp[:, 16:24],
                                               -1e30, gtmp[:, 8:16],
                                               op0=ALU.mult, op1=ALU.add
                                               ).then_inc(S["vvO"], 1)
                        nvo += 1
                        v.wait_ge(S["vvO"], nvo)
                        v.reduce_max(gtmp[:, 1:2], gtmp[:, 8:16], axis=AX
                                     ).then_inc(S["vvO"], 1)
                        nvo += 1
                        vvo_gate.append(nvo)
                        v.wait_ge(S["aaO"], 3 * t + 3)
                        v.tensor_scalar_add(gtmp[:, 3:4], gtmp[:, 2:3], 1.0
                                            ).then_inc(S["vvO"], 1)
                        nvo += 1
                        v.wait_ge(S["vvO"], nvo)
                        v.reciprocal(gtmp[:, 4:5], gtmp[:, 3:4]
                                     ).then_inc(S["vvO"], 1)
                        nvo += 1
                        v.wait_ge(S["vvO"], nvo)
                        v.tensor_mul(gtmp[:, 5:6], gtmp[:, 2:3], gtmp[:, 4:5]
                                     ).then_inc(S["vvO"], 1)
                        nvo += 1
                        v.wait_ge(S["vvO"], nvo)
                        v.tensor_add(rw_all[:, t:t + 1], gtmp[:, 4:5],
                                     gtmp[:, 5:6])

                @blk.scalar
                def _(sc):
                    for t in range(NT):
                        b = t % 2
                        if t >= 2:
                            sc.wait_ge(S["pWo"], 2 * (t - 2) + 2)
                        for c in range(ND):
                            i = t * 32 + c
                            sc.wait_ge(S["pT1"], i + 1)
                            sc.copy(OfmT[:, b * D + c * P:b * D + c * P + P],
                                    tpO[i % 3][:, :]).then_inc(S["aCp1"], 1)
                        sc.wait_ge(S["vvO"], vvo_ln[t])
                        sc.activation(lnstO[:, 6:7], lnstO[:, 5:6], AF.Ln
                                      ).then_inc(S["aaO"], 1)
                        sc.wait_ge(S["aaO"], 3 * t + 1)
                        sc.activation(lnstO[:, 7:8], lnstO[:, 6:7], AF.Exp,
                                      scale=-0.5).then_inc(S["aaO"], 1)
                        if t >= 2:
                            sc.wait_ge(S["sH2w"], 16 * (t - 1))
                            sc.wait_ge(S["pLg"], t - 1)
                        for c in range(ND):
                            i = t * 32 + 16 + c
                            sc.wait_ge(S["pT1"], i + 1)
                            sc.copy(h2T[:, b * D + c * P:b * D + c * P + P],
                                    tpO[i % 3][:, :]).then_inc(S["aCp1"], 1)
                        sc.wait_ge(S["vvO"], vvo_gate[t])
                        sc.activation(gtmp[:, 2:3], gtmp[:, 1:2], AF.Exp
                                      ).then_inc(S["aaO"], 1)

                @blk.gpsimd
                def _(g):
                    for t in range(NT):
                        b = t % 2
                        g.wait_ge(S["vEvO"], 2 * t + 2)
                        g.dma_start(
                            bass.AP(x2d, t * P * D, [[D, P], [1, D]]),
                            x2t[:, b * D:b * D + D]).then_inc(S["sX2w"], 16)
                        g.wait_ge(S["aCp1"], 32 * (t + 1))
                        g.dma_start(
                            bass.AP(h2d, t * P, [[L, P], [P * L, ND], [1, P]]),
                            h2T[:, :].rearrange("p (b c m) -> p b c m",
                                                b=2, c=ND)[:, b],
                        ).then_inc(S["sH2w"], 16)

        if ph("f1"):
         with contextlib.ExitStack() as fc_:
            ef = fc_.enter_context
            h2cF = ef(nc.sbuf_tensor("h2cF", [P, 2 * ND * 512], BF16))
            upst = ef(nc.sbuf_tensor("upst", [P, 2 * ND * 512], BF16))
            upps = [ef(nc.psum_tensor("upA", [P, 512], F32)),
                    ef(nc.psum_tensor("upB", [P, 512], F32))]

            with nc.Block() as blk:
                @blk.sync
                def _(s):
                    s.wait_ge(S["sH2w"], 16 * NT)
                    for tc in range(NTC):
                        if tc >= 2:
                            s.wait_ge(S["pF1"], ND * (tc - 1))
                        b = tc % 2
                        s.dma_start(
                            h2cF[:, :].rearrange("p (b c m) -> p b c m",
                                                 b=2, c=ND)[:, b],
                            bass.AP(h2d, tc * 512,
                                    [[L, P], [P * L, ND], [1, 512]]),
                        ).then_inc(S["sh2F"], 16)

                @blk.tensor
                def _(tn):
                    tn.wait_ge(S["sWB0"], 192)
                    for tc in range(NTC):
                        tn.wait_ge(S["sh2F"], 16 * (tc + 1))
                        b = tc % 2
                        for ee in range(ND):
                            i = tc * ND + ee
                            if i >= 2:
                                tn.wait_ge(S["aRl"], i - 1)
                            for d in range(ND):
                                mm = tn.matmul(
                                    upps[i % 2][:, :],
                                    WB0[:, d * D + ee * P:d * D + ee * P + P],
                                    h2cF[:, b * ND * 512 + d * 512:
                                         b * ND * 512 + d * 512 + 512],
                                    start=(d == 0), stop=(d == ND - 1))
                            mm.then_inc(S["pF1"], 1)

                @blk.scalar
                def _(sc):
                    for tc in range(NTC):
                        b = tc % 2
                        if tc >= 2:
                            sc.wait_ge(S["sUw"], 16 * (tc - 1))
                        for ee in range(ND):
                            i = tc * ND + ee
                            sc.wait_ge(S["pF1"], i + 1)
                            sc.activation(
                                upst[:, b * ND * 512 + ee * 512:
                                     b * ND * 512 + ee * 512 + 512],
                                upps[i % 2][:, :], AF.Relu).then_inc(S["aRl"], 1)

                @blk.gpsimd
                def _(g):
                    wload(g, WB1, wdownT, D, 0, D, S["sWB1"])
                    for tc in range(NTC):
                        b = tc % 2
                        g.wait_ge(S["aRl"], ND * (tc + 1))
                        g.dma_start(
                            bass.AP(upd, tc * 512,
                                    [[L, P], [P * L, ND], [1, 512]]),
                            upst[:, :].rearrange("p (b c m) -> p b c m",
                                                 b=2, c=ND)[:, b],
                        ).then_inc(S["sUw"], 16)

        if ph("f23"):
         with contextlib.ExitStack() as gc_:
            eg = gc_.enter_context
            uptile = eg(nc.sbuf_tensor("uptile", [P, 2 * D], BF16))
            h2tile = eg(nc.sbuf_tensor("h2tile", [P, 2 * D], BF16))
            x2tile = eg(nc.sbuf_tensor("x2tile", [P, 2 * D], F32))
            outt = eg(nc.sbuf_tensor("outt", [P, 2 * D], F32))
            ftmp = eg(nc.sbuf_tensor("ftmp", [P, 1024], F32))
            sh_ps = [eg(nc.psum_tensor("shA", [P, 1024], F32)),
                     eg(nc.psum_tensor("shB", [P, 1024], F32))]
            ex_ps = [eg(nc.psum_tensor("exA", [P, 1024], F32)),
                     eg(nc.psum_tensor("exB", [P, 1024], F32))]

            with nc.Block() as blk:
                @blk.gpsimd
                def _(g):
                    wload(g, WB0, wexp, D, 0, D, S["sWB0"])

                @blk.sync
                def _(s):
                    s.wait_ge(S["sUw"], 16 * NTC)
                    s.wait_ge(S["sX2w"], 16 * NT)
                    for t in range(NT):
                        b = t % 2
                        if t >= 2:
                            s.wait_ge(S["vF23"], 2 * (t - 2) + 2)
                        s.dma_start(
                            uptile[:, :].rearrange("p (b c m) -> p b c m",
                                                   b=2, c=ND)[:, b],
                            bass.AP(upd, t * P,
                                    [[L, P], [P * L, ND], [1, P]]),
                        ).then_inc(S["sF3l"], 16)
                        s.dma_start(
                            h2tile[:, :].rearrange("p (b c m) -> p b c m",
                                                   b=2, c=ND)[:, b],
                            bass.AP(h2d, t * P,
                                    [[L, P], [P * L, ND], [1, P]]),
                        ).then_inc(S["sF3l"], 16)
                        s.dma_start(
                            x2tile[:, b * D:b * D + D],
                            bass.AP(x2d, t * P * D, [[D, P], [1, D]]),
                        ).then_inc(S["sF3l"], 16)

                @blk.tensor
                def _(tn):
                    tn.wait_ge(S["sWB0"], 256)
                    for t in range(NT):
                        b = t % 2
                        tn.wait_ge(S["sF3l"], 48 * (t + 1))
                        for half in range(2):
                            if t >= 1:
                                tn.wait_ge(S["vF23"], 2 * (t - 1) + half + 1)
                            for sl in range(2):
                                for o in range(ND):
                                    tn.matmul(
                                        sh_ps[half][:, sl * 512:sl * 512 + 512],
                                        uptile[:, b * D + o * P:b * D + o * P + P],
                                        WB1[:, o * D + half * 1024 + sl * 512:
                                            o * D + half * 1024 + sl * 512 + 512],
                                        start=(o == 0), stop=(o == ND - 1))
                            for sl in range(2):
                                for d in range(ND):
                                    mm = tn.matmul(
                                        ex_ps[half][:, sl * 512:sl * 512 + 512],
                                        h2tile[:, b * D + d * P:b * D + d * P + P],
                                        WB0[:, d * D + half * 1024 + sl * 512:
                                            d * D + half * 1024 + sl * 512 + 512],
                                        start=(d == 0), stop=(d == ND - 1))
                            mm.then_inc(S["pF2h"], 1)

                @blk.vector
                def _(v):
                    for t in range(NT):
                        b = t % 2
                        for half in range(2):
                            v.wait_ge(S["pF2h"], 2 * t + half + 1)
                            if half == 0 and t >= 2:
                                v.wait_ge(S["sOutW"], 16 * (t - 1))
                            v.scalar_tensor_tensor(
                                ftmp[:, :], ex_ps[half][:, :],
                                rw_all[:, t:t + 1],
                                x2tile[:, b * D + half * 1024:
                                       b * D + half * 1024 + 1024],
                                op0=ALU.mult, op1=ALU.add)
                            v.tensor_add(
                                outt[:, b * D + half * 1024:
                                     b * D + half * 1024 + 1024],
                                ftmp[:, :], sh_ps[half][:, :]
                            ).then_inc(S["vF23"], 1)

                @blk.scalar
                def _(sc):
                    for t in range(NT):
                        b = t % 2
                        sc.wait_ge(S["vF23"], 2 * t + 2)
                        sc.dma_start(
                            bass.AP(out, t * P * D, [[D, P], [1, D]]),
                            outt[:, b * D:b * D + D]).then_inc(S["sOutW"], 16)
                    sc.wait_ge(S["sOutW"], 16 * NT)

        if tap_outs:
            with nc.Block() as blk:
                @blk.sync
                def _(s):
                    for sem, tot in [("sH1w", 16 * NT), ("sQw", 16 * H * NTC),
                                     ("sKw", 16 * H * NTC),
                                     ("sVw", 16 * NTC * 16),
                                     ("sOw", 16 * H * NTC), ("sX2w", 16 * NT),
                                     ("sH2w", 16 * NT), ("sUw", 16 * NTC)]:
                        gate = {"sH1w": "ln1", "sQw": "q", "sKw": "k",
                                "sVw": "v", "sOw": "attn", "sX2w": "o",
                                "sH2w": "o", "sUw": "f1"}[sem]
                        if ph(gate):
                            s.wait_ge(S[sem], tot)
                    ntap = 0
                    for dst, src in tap_outs:
                        rows, cols = src.shape
                        s.dma_start(
                            bass.AP(dst, 0, [[cols, rows], [1, cols]]),
                            bass.AP(src, 0, [[cols, rows], [1, cols]]),
                        ).then_inc(S["sTap"], 16)
                        ntap += 1
                    s.wait_ge(S["sTap"], 16 * ntap)

    return nc


def host_inputs(L0_attn_x, L0_attn_W_qkv, L0_attn_W_o, L0_ffn_W_up,
                L0_ffn_W_down, L0_ffn_W_gate, L0_ffn_W_expert):
    bf16 = ml_dtypes.bfloat16
    x = np.ascontiguousarray(np.asarray(L0_attn_x, dtype=np.float32))
    wqkv = np.asarray(L0_attn_W_qkv, dtype=np.float32)
    pp = np.concatenate([np.arange(0, P, 2), np.arange(1, P, 2)])
    rows = np.arange(3 * D)
    for sec in (0, D):
        for h in range(H):
            base = sec + h * P
            rows[base:base + P] = base + pp
    wqkvT = np.ascontiguousarray(wqkv[rows].T)
    woT = np.ascontiguousarray(np.asarray(L0_attn_W_o, dtype=np.float32).T)
    wupT = np.ascontiguousarray(np.asarray(L0_ffn_W_up, dtype=np.float32).T)
    wdownT = np.ascontiguousarray(np.asarray(L0_ffn_W_down, dtype=np.float32).T)
    wexp = np.ascontiguousarray(np.asarray(L0_ffn_W_expert, dtype=np.float32))
    wgT = np.ascontiguousarray(np.asarray(L0_ffn_W_gate, dtype=np.float32).T)

    inv = (1.0 / (10000.0 ** (np.arange(0, P, 2, dtype=np.float32) / P))
           ).astype(np.float32)
    ang = np.arange(L, dtype=np.float32)[:, None] * inv[None, :]
    cosv = np.cos(ang).T.astype(np.float32)
    sinv = np.sin(ang).T.astype(np.float32)
    costab = np.ascontiguousarray(np.vstack([cosv, cosv]))
    nsintab = np.ascontiguousarray(np.vstack([-sinv, sinv]))

    identv = np.eye(P, dtype=bf16)
    masks = np.zeros((P, 4 * 512), dtype=np.float32)
    for j in range(4):
        masks[:, j * 512:(j + 1) * 512] = (
            (np.arange(512)[None, :] - P * j) >= np.arange(P)[:, None]
        ).astype(np.float32)
    masksv = masks.astype(bf16)

    shared = {"wqkvT": wqkvT, "woT": woT, "wupT": wupT, "wdownT": wdownT,
              "wexp": wexp, "wgT": wgT, "costab": costab, "nsintab": nsintab,
              "identin": identv, "masksin": masksv}
    in_maps = []
    for b in range(NCORES):
        m = dict(shared)
        m["x"] = np.ascontiguousarray(x[b])
        in_maps.append(m)
    return in_maps


def kernel(**inputs):
    in_maps = host_inputs(**inputs)
    nc = build_nc()
    res = run_bass_kernel_spmd(nc, in_maps, core_ids=list(range(NCORES)))
    return np.stack([np.asarray(r["out"], dtype=np.float32)
                     for r in res.results], axis=0)


# version 03
# speedup vs baseline: 1.0427x; 1.0187x over previous
import contextlib

import numpy as np
import ml_dtypes

import concourse.bass as bass
import concourse.mybir as mybir
from concourse.bass_utils import run_bass_kernel_spmd

F32 = mybir.dt.float32
BF16 = mybir.dt.bfloat16
AF = mybir.ActivationFunctionType
ALU = mybir.AluOpType
AX = mybir.AxisListType.X

P = 128
D = 2048
L = 4096
H = 16
ND = D // P
NT = L // P
NTC = L // 512
SCALE = float(1.0 / np.sqrt(128.0))
EPS = 1e-5
NCORES = 8


PH_ORDER = ["ln1", "q", "k", "v", "bias", "attn", "o", "f1", "f23"]


def build_nc(taps=(), upto="f23"):
    import os as _os
    _last = PH_ORDER.index(upto)
    _skip = set(x for x in _os.environ.get("SKIP_PH", "").split(",") if x)

    def ph(name):
        return PH_ORDER.index(name) <= _last and name not in _skip

    nc = bass.Bass("TRN2", target_bir_lowering=False)

    x_in = nc.dram_tensor("x", [L, D], F32, kind="ExternalInput")
    wqkvT = nc.dram_tensor("wqkvT", [D, 3 * D], F32, kind="ExternalInput")
    woT = nc.dram_tensor("woT", [D, D], F32, kind="ExternalInput")
    wupT = nc.dram_tensor("wupT", [D, D], F32, kind="ExternalInput")
    wdownT = nc.dram_tensor("wdownT", [D, D], F32, kind="ExternalInput")
    wexp = nc.dram_tensor("wexp", [D, D], F32, kind="ExternalInput")
    wgT = nc.dram_tensor("wgT", [D, 8], F32, kind="ExternalInput")
    costab = nc.dram_tensor("costab", [P, L], F32, kind="ExternalInput")
    nsintab = nc.dram_tensor("nsintab", [P, L], F32, kind="ExternalInput")
    identin = nc.dram_tensor("identin", [P, P], BF16, kind="ExternalInput")
    masksin = nc.dram_tensor("masksin", [P, 4 * 512], BF16, kind="ExternalInput")
    out = nc.dram_tensor("out", [L, D], F32, kind="ExternalOutput")

    h1d = nc.dram_tensor("h1d", [D, L], BF16)
    qd = nc.dram_tensor("qd", [D, L], BF16)
    kd = nc.dram_tensor("kd", [D, L], BF16)
    vt = nc.dram_tensor("vt", [L, D], BF16)
    otok = nc.dram_tensor("otok", [L, D], BF16)
    x2d = nc.dram_tensor("x2d", [L, D], F32)
    h2d = nc.dram_tensor("h2d", [D, L], BF16)
    upd = nc.dram_tensor("upd", [D, L], BF16)
    scr_map = {"h1d": h1d, "qd": qd, "kd": kd, "vt": vt, "otok": otok,
               "x2d": x2d, "h2d": h2d, "upd": upd}
    tap_outs = [(nc.dram_tensor("tap_" + n, list(scr_map[n].shape),
                                scr_map[n].dtype, kind="ExternalOutput"),
                 scr_map[n]) for n in taps]

    with contextlib.ExitStack() as ctx:
        e = ctx.enter_context

        ident = e(nc.sbuf_tensor("ident", [P, P], BF16))
        wg_sb = e(nc.sbuf_tensor("wg_sb", [P, ND * 8], BF16))
        ones_sb = e(nc.sbuf_tensor("ones_sb", [P, P], BF16))
        qn2 = e(nc.sbuf_tensor("qn2", [1, H], F32))
        kn2 = e(nc.sbuf_tensor("kn2", [1, H], F32))
        biasC = e(nc.sbuf_tensor("biasC", [P, H], F32))
        rw_all = e(nc.sbuf_tensor("rw_all", [P, NT], F32))
        smallA = e(nc.sbuf_tensor("smallA", [1, 16], F32))
        smallB = e(nc.sbuf_tensor("smallB", [1, 16], BF16))

        WB0 = e(nc.sbuf_tensor("WB0", [P, ND * D], BF16))
        WB1 = e(nc.sbuf_tensor("WB1", [P, ND * D], BF16))

        S = {}
        for name in [
            "sWB0", "sWB1", "sCst", "gOnes",
            "sX", "vH", "vvL", "aaL", "pTL", "vCpL", "sH1w",
            "sh1Q", "pMMQ", "vPQ", "vStQ", "vSqQ", "pQNQ", "vQNQ", "sQw", "vvQ", "pTCQ",
            "sh1K", "pMMK", "vPK", "vStK", "vSqK", "pQNK", "vQNK", "sKw", "vvK", "pTCK",
            "sh1V", "pMMV", "aVV", "sVw",
            "vvB", "aaB", "pB",
            "sKV", "sQl", "pS", "aExp", "vMask", "pPt", "pOd", "vOev", "sOw",
            "pQd", "pHd", "vvA", "sMsk",
            "sOl", "pT1", "aCp1", "pWo", "vEvO", "pLg", "vGt", "sX2w",
            "sH2w", "vvO", "aaO", "vH2",
            "sh2F", "pF1", "aRl", "sUw", "pTCF",
            "sF3l", "pF2h", "vF23", "sOutW",
            "sTap",
        ]:
            S[name] = e(nc.semaphore(name))

        def wload(g, dst, src, src_ld, col0, ncols, sem):
            for part in range(4):
                dst_ap = dst[:, :].rearrange("p (c m) -> p c m", c=ND)[
                    :, 4 * part:4 * part + 4, 0:ncols]
                src_ap = bass.AP(
                    src, (4 * part) * P * src_ld + col0,
                    [[src_ld, P], [P * src_ld, 4], [1, ncols]])
                g.dma_start(dst_ap, src_ap).then_inc(sem, 16)

        with contextlib.ExitStack() as lc:
            el = lc.enter_context
            Xt = el(nc.sbuf_tensor("XtL", [P, 2 * D], F32))
            scr = el(nc.sbuf_tensor("scrL", [P, D], F32))
            hbf = el(nc.sbuf_tensor("hbfL", [P, 2 * D], BF16))
            h1st = el(nc.sbuf_tensor("h1stL", [P, 2 * D], BF16))
            lnst = el(nc.sbuf_tensor("lnstL", [P, 16], F32))
            tp = [el(nc.psum_tensor("tp%d" % i_, [P, P], BF16))
                  for i_ in range(4)]

            with nc.Block() as blk:
                @blk.gpsimd
                def _(g):
                    g.memset(ones_sb[:, :], 1.0)
                    g.memset(qn2[:, :], 0.0)
                    g.memset(kn2[:, :], 0.0).then_inc(S["gOnes"], 1)
                    wload(g, WB0, wqkvT, 3 * D, 0, D, S["sWB0"])
                    wload(g, WB1, wqkvT, 3 * D, D, D, S["sWB1"])
                    g.dma_start(
                        wg_sb[:, :].rearrange("p (c m) -> p c m", c=ND),
                        bass.AP(wgT, 0, [[8, P], [P * 8, ND], [1, 8]]),
                    ).then_inc(S["sCst"], 16)
                    for t in range(NT):
                        b = t % 2
                        g.wait_ge(S["vCpL"], ND * (t + 1))
                        g.dma_start(
                            bass.AP(h1d, t * P, [[L, P], [P * L, ND], [1, P]]),
                            h1st[:, :].rearrange("p (b c m) -> p b c m",
                                                 b=2, c=ND)[:, b],
                        ).then_inc(S["sH1w"], 16)

                @blk.sync
                def _(s):
                    s.dma_start(ident[:, :], identin[:, :]).then_inc(S["sCst"], 16)
                    for t in range(NT):
                        if t >= 2:
                            s.wait_ge(S["vH"], t - 1)
                        s.dma_start(
                            Xt[:, (t % 2) * D:(t % 2) * D + D],
                            bass.AP(x_in, t * P * D, [[D, P], [1, D]]),
                        ).then_inc(S["sX"], 16)

                def emit_ln_copies(v, t):
                    b = t % 2
                    if t >= 2:
                        v.wait_ge(S["sH1w"], 16 * (t - 1))
                    for c in range(ND):
                        i = t * ND + c
                        v.wait_ge(S["pTL"], i + 1)
                        v.tensor_copy(h1st[:, b * D + c * P:b * D + c * P + P],
                                      tp[i % 4][:, :]).then_inc(S["vCpL"], 1)

                @blk.vector
                def _(v):
                    n = 0
                    for t in range(NT):
                        b = t % 2
                        xa = Xt[:, b * D:b * D + D]
                        v.wait_ge(S["sX"], 16 * (t + 1))
                        if t >= 2:
                            v.wait_ge(S["pTL"], ND * (t - 1))
                        v.scalar_tensor_tensor(
                            scr[:, :], xa, 1.0, xa, op0=ALU.mult, op1=ALU.mult,
                            accum_out=lnst[:, 1:2]).then_inc(S["vvL"], 1)
                        v.reduce_sum(lnst[:, 0:1], xa, axis=AX
                                     ).then_inc(S["vvL"], 1)
                        n += 2
                        v.wait_ge(S["vvL"], n)
                        v.tensor_scalar_mul(lnst[:, 2:3], lnst[:, 0:1], 1.0 / D
                                            ).then_inc(S["vvL"], 1)
                        v.tensor_scalar(lnst[:, 3:4], lnst[:, 1:2], 1.0 / D,
                                        EPS, op0=ALU.mult, op1=ALU.add
                                        ).then_inc(S["vvL"], 1)
                        n += 2
                        v.wait_ge(S["vvL"], n)
                        v.tensor_mul(lnst[:, 4:5], lnst[:, 2:3], lnst[:, 2:3]
                                     ).then_inc(S["vvL"], 1)
                        n += 1
                        v.wait_ge(S["vvL"], n)
                        v.tensor_sub(lnst[:, 5:6], lnst[:, 3:4], lnst[:, 4:5]
                                     ).then_inc(S["vvL"], 1)
                        n += 1
                        v.wait_ge(S["aaL"], 2 * (t + 1))
                        v.tensor_scalar(lnst[:, 8:9], lnst[:, 2:3],
                                        lnst[:, 7:8], -1.0,
                                        op0=ALU.mult, op1=ALU.mult
                                        ).then_inc(S["vvL"], 1)
                        n += 1
                        v.wait_ge(S["vvL"], n)
                        v.tensor_scalar(
                            hbf[:, b * D:b * D + D], xa,
                            lnst[:, 7:8], lnst[:, 8:9],
                            op0=ALU.mult, op1=ALU.add).then_inc(S["vH"], 1)
                        if t >= 1:
                            emit_ln_copies(v, t - 1)
                    emit_ln_copies(v, NT - 1)

                @blk.scalar
                def _(sc):
                    for t in range(NT):
                        sc.wait_ge(S["vvL"], 7 * t + 6)
                        sc.activation(lnst[:, 6:7], lnst[:, 5:6], AF.Ln
                                      ).then_inc(S["aaL"], 1)
                        sc.wait_ge(S["aaL"], 2 * t + 1)
                        sc.activation(lnst[:, 7:8], lnst[:, 6:7], AF.Exp,
                                      scale=-0.5).then_inc(S["aaL"], 1)

                @blk.tensor
                def _(tn):
                    tn.wait_ge(S["sCst"], 32)
                    for t in range(NT):
                        b = t % 2
                        tn.wait_ge(S["vH"], t + 1)
                        for c in range(ND):
                            i = t * ND + c
                            if i >= 4:
                                tn.wait_ge(S["vCpL"], i - 3)
                            tn.transpose(
                                tp[i % 4][:, :],
                                hbf[:, b * D + c * P:b * D + c * P + P],
                                ident[:, :]).then_inc(S["pTL"], 1)

        def qk_phase(tag, col0, WB, sWBn, wbtgt, sh1, pMM, vP, vSt, vSq, pQN,
                     vQN, sW, pTC, stats, dst, prefetch):
            with contextlib.ExitStack() as qc_:
                eq = qc_.enter_context
                h1c = eq(nc.sbuf_tensor("h1c" + tag, [P, 2 * ND * 512], BF16))
                cosS = eq(nc.sbuf_tensor("cosS" + tag, [P, 2 * 512], F32))
                nsinS = eq(nc.sbuf_tensor("nsinS" + tag, [P, 2 * 512], F32))
                ropeA = eq(nc.sbuf_tensor("ropeA" + tag, [P, 512], F32))
                ropeB = eq(nc.sbuf_tensor("ropeB" + tag, [P, 512], F32))
                qst = eq(nc.sbuf_tensor("qst" + tag, [P, 2 * 512], BF16))
                sqb = eq(nc.sbuf_tensor("sqb" + tag, [P, 2 * 512], BF16))
                qp = [eq(nc.psum_tensor("qpA" + tag, [P, 512], F32)),
                      eq(nc.psum_tensor("qpB" + tag, [P, 512], F32))]
                qn = [eq(nc.psum_tensor("qnA" + tag, [1, 512], F32)),
                      eq(nc.psum_tensor("qnB" + tag, [1, 512], F32))]
                NI = NTC * H

                with nc.Block() as blk:
                    @blk.sync
                    def _(s):
                        for tc in range(NTC):
                            if tc >= 2:
                                s.wait_ge(pMM, H * (tc - 1))
                            s.wait_ge(S["sH1w"], 16 * 4 * (tc + 1))
                            b = tc % 2
                            s.dma_start(
                                h1c[:, :].rearrange("p (b c m) -> p b c m",
                                                    b=2, c=ND)[:, b],
                                bass.AP(h1d, tc * 512,
                                        [[L, P], [P * L, ND], [1, 512]]),
                            ).then_inc(sh1, 16)
                            s.dma_start(cosS[:, b * 512:b * 512 + 512],
                                        costab[:, tc * 512:tc * 512 + 512]
                                        ).then_inc(sh1, 16)
                            s.dma_start(nsinS[:, b * 512:b * 512 + 512],
                                        nsintab[:, tc * 512:tc * 512 + 512]
                                        ).then_inc(sh1, 16)

                    def norm_mm(tn, i):
                        tn.wait_ge(vSq, i + 1)
                        if i >= 2:
                            tn.wait_ge(vQN, i - 1)
                        tn.matmul(qn[i % 2][:, :], ones_sb[:, 0:1],
                                  sqb[:, (i % 2) * 512:(i % 2) * 512 + 512],
                                  start=True, stop=True).then_inc(pQN, 1)

                    @blk.tensor
                    def _(tn):
                        tn.wait_ge(sWBn, wbtgt)
                        tn.wait_ge(S["gOnes"], 1)
                        for tc in range(NTC):
                            tn.wait_ge(sh1, 48 * (tc + 1))
                            b = tc % 2
                            for ee in range(H):
                                i = tc * H + ee
                                if i >= 2:
                                    tn.wait_ge(vP, i - 1)
                                for d in range(ND):
                                    mm = tn.matmul(
                                        qp[i % 2][:, :],
                                        WB[:, d * D + ee * P:d * D + ee * P + P],
                                        h1c[:, b * ND * 512 + d * 512:
                                            b * ND * 512 + d * 512 + 512],
                                        start=(d == 0), stop=(d == ND - 1))
                                mm.then_inc(pMM, 1)
                                if i >= 1:
                                    norm_mm(tn, i - 1)
                        norm_mm(tn, NI - 1)

                    def qk_norm_red(v, j, nv):
                        eej = j % H
                        v.wait_ge(pQN, j + 1)
                        v.reduce_max(smallA[0:1, 0:1], qn[j % 2][0:1, :],
                                     axis=AX).then_inc(S["vv" + tag], 1)
                        nv += 1
                        v.wait_ge(S["vv" + tag], nv)
                        v.tensor_max(stats[0:1, eej:eej + 1],
                                     stats[0:1, eej:eej + 1],
                                     smallA[0:1, 0:1]).then_inc(vQN, 1)
                        return nv

                    @blk.vector
                    def _(v):
                        nv = 0
                        for tc in range(NTC):
                            b = tc % 2
                            for ee in range(H):
                                i = tc * H + ee
                                sp = i % 2
                                v.wait_ge(pMM, i + 1)
                                cs = cosS[:, b * 512:b * 512 + 512]
                                ns = nsinS[:, b * 512:b * 512 + 512]
                                v.tensor_mul(ropeA[:, :], qp[sp][:, :], cs)
                                v.tensor_mul(ropeB[0:64, :],
                                             qp[sp][64:128, :], ns[0:64, :])
                                v.tensor_mul(ropeB[64:128, :],
                                             qp[sp][0:64, :], ns[64:128, :]
                                             ).then_inc(vP, 1)
                                if i >= 2:
                                    v.wait_ge(sW, 16 * (i - 1))
                                v.tensor_add(qst[:, sp * 512:sp * 512 + 512],
                                             ropeA[:, :], ropeB[:, :]
                                             ).then_inc(vSt, 1)
                                if i >= 2:
                                    v.wait_ge(pQN, i - 1)
                                v.tensor_mul(sqb[:, sp * 512:sp * 512 + 512],
                                             qst[:, sp * 512:sp * 512 + 512],
                                             qst[:, sp * 512:sp * 512 + 512]
                                             ).then_inc(vSq, 1)
                                if i >= 1:
                                    nv = qk_norm_red(v, i - 1, nv)
                        qk_norm_red(v, NI - 1, nv)

                    @blk.scalar
                    def _(sc):
                        for tc in range(NTC):
                            for ee in range(H):
                                i = tc * H + ee
                                sp = i % 2
                                sc.wait_ge(vSt, i + 1)
                                sc.dma_start(
                                    bass.AP(dst, (ee * P) * L + tc * 512,
                                            [[L, P], [1, 512]]),
                                    qst[:, sp * 512:sp * 512 + 512],
                                ).then_inc(sW, 16)

                    @blk.gpsimd
                    def _(g):
                        if prefetch is not None:
                            pf_dst, pf_src, pf_ld, pf_col, pf_sem = prefetch
                            wload(g, pf_dst, pf_src, pf_ld, pf_col, D, pf_sem)


        if ph("q"):
            qk_phase("Q", 0, WB0, S["sWB0"], 64, S["sh1Q"], S["pMMQ"], S["vPQ"],
                     S["vStQ"], S["vSqQ"], S["pQNQ"], S["vQNQ"], S["sQw"],
                     S["pTCQ"], qn2, qd, None)
        if ph("k"):
            qk_phase("K", D, WB1, S["sWB1"], 64, S["sh1K"], S["pMMK"], S["vPK"],
                     S["vStK"], S["vSqK"], S["pQNK"], S["vQNK"], S["sKw"],
                     S["pTCK"], kn2, kd,
                     (WB0, wqkvT, 3 * D, 2 * D, S["sWB0"]))

        if ph("v"):
         with contextlib.ExitStack() as vc_:
            ev2 = vc_.enter_context
            h1cV = ev2(nc.sbuf_tensor("h1cV", [P, 2 * ND * 512], BF16))
            vst = ev2(nc.sbuf_tensor("vst", [P, 2 * 512], BF16))
            vp = [ev2(nc.psum_tensor("vpA", [P, 512], F32)),
                  ev2(nc.psum_tensor("vpB", [P, 512], F32))]

            with nc.Block() as blk:
                @blk.sync
                def _(s):
                    for tc in range(NTC):
                        if tc >= 2:
                            s.wait_ge(S["pMMV"], 16 * (tc - 1))
                        b = tc % 2
                        s.dma_start(
                            h1cV[:, :].rearrange("p (b c m) -> p b c m",
                                                 b=2, c=ND)[:, b],
                            bass.AP(h1d, tc * 512,
                                    [[L, P], [P * L, ND], [1, 512]]),
                        ).then_inc(S["sh1V"], 16)

                @blk.tensor
                def _(tn):
                    tn.wait_ge(S["sWB0"], 128)
                    for tc in range(NTC):
                        tn.wait_ge(S["sh1V"], 16 * (tc + 1))
                        b = tc % 2
                        for t4 in range(4):
                            for e4 in range(4):
                                i = (tc * 4 + t4) * 4 + e4
                                if i >= 2:
                                    tn.wait_ge(S["aVV"], i - 1)
                                for d in range(ND):
                                    mm = tn.matmul(
                                        vp[i % 2][:, :],
                                        h1cV[:, b * ND * 512 + d * 512 + t4 * P:
                                             b * ND * 512 + d * 512 + t4 * P + P],
                                        WB0[:, d * D + e4 * 512:
                                            d * D + e4 * 512 + 512],
                                        start=(d == 0), stop=(d == ND - 1))
                                mm.then_inc(S["pMMV"], 1)

                @blk.scalar
                def _(sc):
                    for i in range(NTC * 16):
                        tc, rem = divmod(i, 16)
                        t4, e4 = divmod(rem, 4)
                        sp = i % 2
                        sc.wait_ge(S["pMMV"], i + 1)
                        if i >= 2:
                            sc.wait_ge(S["sVw"], 16 * (i - 1))
                        sc.copy(vst[:, sp * 512:sp * 512 + 512], vp[sp][:, :]
                                ).then_inc(S["aVV"], 1)
                        sc.wait_ge(S["aVV"], i + 1)
                        sc.dma_start(
                            bass.AP(vt, (tc * 512 + t4 * P) * D + e4 * 512,
                                    [[D, P], [1, 512]]),
                            vst[:, sp * 512:sp * 512 + 512],
                        ).then_inc(S["sVw"], 16)

                @blk.gpsimd
                def _(g):
                    wload(g, WB1, woT, D, 0, D, S["sWB1"])

        if ph("bias"):
         with contextlib.ExitStack() as bc_:
            bps = bc_.enter_context(nc.psum_tensor("bps", [P, H], F32))
            with nc.Block() as blk:
                @blk.vector
                def _(v):
                    v.tensor_mul(smallA[0:1, 0:16], qn2[0:1, :], kn2[0:1, :]
                                 ).then_inc(S["vvB"], 1)
                    v.wait_ge(S["vvB"], 1)
                    v.tensor_copy(smallB[0:1, 0:16], smallA[0:1, 0:16]
                                  ).then_inc(S["vvB"], 1)
                    v.wait_ge(S["aaB"], 2)
                    v.tensor_scalar(biasC[:, :], biasC[:, :], -SCALE, -1.0,
                                    op0=ALU.mult, op1=ALU.add
                                    ).then_inc(S["vvB"], 1)

                @blk.tensor
                def _(tn):
                    tn.wait_ge(S["vvB"], 2)
                    for h in range(H):
                        mm = tn.matmul(bps[:, h:h + 1], ones_sb[0:1, :],
                                       smallB[0:1, h:h + 1], start=True,
                                       stop=True)
                    mm.then_inc(S["pB"], 1)

                @blk.scalar
                def _(sc):
                    sc.wait_ge(S["pB"], 1)
                    sc.activation(biasC[:, :], bps[:, :], AF.Ln
                                  ).then_inc(S["aaB"], 1)
                    sc.wait_ge(S["aaB"], 1)
                    sc.activation(biasC[:, :], biasC[:, :], AF.Exp, scale=0.5
                                  ).then_inc(S["aaB"], 1)

        import os
        _astage = int(os.environ.get("ATT_STAGE", "4"))
        _njq = int(os.environ.get("ATT_J", str(H * NTC)))
        ahq = [(h, qc) for h in range(H) for qc in range(NTC)][:_njq]
        apairs = []
        pair_of = {}
        diag_ord = {}
        for j, (h, qc) in enumerate(ahq):
            for pp in range(2 * qc + 2):
                pair_of[(j, pp)] = len(apairs)
                apairs.append((j, h, qc, pp, pp >= 2 * qc))
        ndi = 0
        for g, (j, h, qc, pp, diag) in enumerate(apairs):
            if diag:
                ndi += 1
                diag_ord[g] = ndi
        cum_pairs = [0] * len(ahq)
        _acc = 0
        for j, (h, qc) in enumerate(ahq):
            _acc += 2 * qc + 2
            cum_pairs[j] = _acc

        VW = 132
        if ph("attn"):
         with contextlib.ExitStack() as ac_:
            ea = ac_.enter_context
            Kb = ea(nc.sbuf_tensor("Kb", [P, 2 * L], BF16))
            Vb = ea(nc.sbuf_tensor("Vb", [P, 2 * 32 * VW], BF16))
            Qb = ea(nc.sbuf_tensor("Qb", [P, 2 * 512], BF16))
            Pt = ea(nc.sbuf_tensor("Pt", [P, 2 * 1024], BF16))
            masks_sb = ea(nc.sbuf_tensor("masks_sb", [P, 4 * 512], BF16))
            biasX = ea(nc.sbuf_tensor("biasX", [P, H], F32))
            Otst = ea(nc.sbuf_tensor("Otst", [P, 2 * 512], BF16))
            rcA = ea(nc.sbuf_tensor("rcA", [P, 8], F32))
            OtF = ea(nc.sbuf_tensor("OtF", [P, 4 * 132], F32))
            Sps = [ea(nc.psum_tensor("SpsA", [P, 1024], F32)),
                   ea(nc.psum_tensor("SpsB", [P, 1024], F32))]
            Ops = [ea(nc.psum_tensor("Ops%d" % qs, [P, 512], F32))
                   for qs in range(4)]

            with nc.Block() as blk:
                @blk.gpsimd
                def _(g):
                    g.memset(Vb[:, :].rearrange("p (n c) -> p n c", c=VW)
                             [:, :, 128:129], 1.0).then_inc(S["gOnes"], 1)
                    wload(g, WB0, wupT, D, 0, D, S["sWB0"])
                    for j, (h, qc) in enumerate(ahq):
                        if _astage < 4 or float(os.environ.get("ATT_EVAC", "3")) < 3:
                            break
                        g.wait_ge(S["vOev"], j + 1)
                        g.dma_start(
                            bass.AP(otok, (qc * 512) * D + h * P,
                                    [[D, P], [P * D, 4], [1, P]]),
                            Otst[:, :].rearrange("p (b q m) -> p b q m",
                                                 b=2, q=4)[:, j % 2],
                        ).then_inc(S["sOw"], 16)

                _nheads = (len(ahq) + NTC - 1) // NTC
                @blk.sync
                def _(s):
                    s.dma_start(masks_sb[:, :], masksin[:, :]
                                ).then_inc(S["sMsk"], 16)
                    if ph("k"):
                        s.wait_ge(S["sKw"], 16 * H * NTC)
                    if ph("v"):
                        s.wait_ge(S["sVw"], 16 * NTC * 16)
                    if ph("q"):
                        s.wait_ge(S["sQw"], 16 * H * NTC)
                    for h in range(_nheads):
                        hb = h % 2
                        if h >= 2:
                            s.wait_ge(S["pPt"], cum_pairs[(h - 1) * NTC - 1])
                        s.dma_start(
                            Kb[:, hb * L:hb * L + L],
                            bass.AP(kd, (h * P) * L, [[L, P], [1, L]]),
                        ).then_inc(S["sKV"], 16)
                        s.dma_start(
                            Vb[:, :].rearrange("p (b n c) -> p b n c",
                                               b=2, n=32)[:, hb, :, 0:128],
                            bass.AP(vt, h * P, [[D, P], [P * D, 32], [1, P]]),
                        ).then_inc(S["sKV"], 16)
                        for qc in range(NTC):
                            j = h * NTC + qc
                            if j >= len(ahq):
                                break
                            if j >= 2:
                                s.wait_ge(S["pS"], cum_pairs[j - 2])
                            s.dma_start(
                                Qb[:, (j % 2) * 512:(j % 2) * 512 + 512],
                                bass.AP(qd, (h * P) * L + qc * 512,
                                        [[L, P], [1, 512]]),
                            ).then_inc(S["sQl"], 16)

                def emit_pv(tn, j, h, qc, pp):
                    if _astage < 3:
                        return
                    g = pair_of[(j, pp)]
                    sp = g % 2
                    hb = h % 2
                    npair = 2 * qc + 2
                    last = pp == npair - 1
                    if _astage >= 2 and pp >= 2 * qc:
                        tn.wait_ge(S["vMask"], diag_ord[g])
                    else:
                        tn.wait_ge(S["aExp"], g + 1)
                    if _astage >= 4 and pp == 0 and j >= 1:
                        tn.wait_ge(S["vOev"], j)
                    for half in range(2):
                        kb = 2 * pp + half
                        for qs in range(4):
                            mm = tn.matmul(
                                Ops[qs][:, 0:129],
                                Pt[:, sp * 1024 + half * 512 + qs * P:
                                   sp * 1024 + half * 512 + qs * P + P],
                                Vb[:, hb * 32 * VW + kb * VW:
                                   hb * 32 * VW + kb * VW + 129],
                                start=(kb == 0), stop=(kb == 2 * npair - 1))
                    mm.then_inc(S["pPt"], 1)

                @blk.tensor
                def _(tn):
                    tn.wait_ge(S["gOnes"], (2 if ph("ln1") else 1))
                    for j, (h, qc) in enumerate(ahq):
                        hb = h % 2
                        tn.wait_ge(S["sKV"], 32 * (h + 1))
                        tn.wait_ge(S["sQl"], 16 * (j + 1))
                        npair = 2 * qc + 2
                        for pp in range(npair):
                            g = pair_of[(j, pp)]
                            sp = g % 2
                            if _astage >= 1 and g >= 2:
                                tn.wait_ge(S["aExp"], g - 1)
                            for half in range(2):
                                kb = 2 * pp + half
                                mm = tn.matmul(
                                    Sps[sp][:, half * 512:half * 512 + 512],
                                    Kb[:, hb * L + kb * P:hb * L + kb * P + P],
                                    Qb[:, (j % 2) * 512:(j % 2) * 512 + 512],
                                    start=True, stop=True)
                            mm.then_inc(S["pS"], 1)
                            if pp >= 1:
                                emit_pv(tn, j, h, qc, pp - 1)
                        emit_pv(tn, j, h, qc, npair - 1)

                @blk.scalar
                def _(sc):
                    for g, (j, h, qc, pp, diag) in enumerate(apairs):
                        if _astage < 1:
                            break
                        sp = g % 2
                        sc.wait_ge(S["pS"], g + 1)
                        if _astage >= 3 and g >= 2:
                            sc.wait_ge(S["pPt"], g - 1)
                        _bias = (0.0 if os.environ.get("ATT_NOBIAS")
                                 else biasX[:, h:h + 1])
                        if g == 0:
                            sc.wait_ge(S["vvA"], 1)
                        sc.activation(Pt[:, sp * 1024:sp * 1024 + 1024],
                                      Sps[sp][:, :], AF.Exp,
                                      bias=_bias, scale=SCALE
                                      ).then_inc(S["aExp"], 1)

                @blk.vector
                def _(v):
                    v.tensor_copy(biasX[:, :], biasC[:, :]).then_inc(S["vvA"], 1)
                    v.wait_ge(S["sMsk"], 16)
                    nva = 1
                    for j, (h, qc) in enumerate(ahq):
                        if _astage >= 2:
                            for pp in (2 * qc, 2 * qc + 1):
                                g = pair_of[(j, pp)]
                                sp = g % 2
                                v.wait_ge(S["aExp"], g + 1)
                                for half in range(2):
                                    jd = 2 * (pp - 2 * qc) + half
                                    mul = v.tensor_mul(
                                        Pt[:, sp * 1024 + half * 512:
                                           sp * 1024 + half * 512 + 512],
                                        Pt[:, sp * 1024 + half * 512:
                                           sp * 1024 + half * 512 + 512],
                                        masks_sb[:, jd * 512:jd * 512 + 512])
                                mul.then_inc(S["vMask"], 1)
                        if _astage < 4:
                            continue
                        _evac = float(os.environ.get("ATT_EVAC", "3"))
                        v.wait_ge(S["pPt"], cum_pairs[j])
                        if _evac >= 3 and j >= 2:
                            v.wait_ge(S["sOw"], 16 * (j - 1))
                        if _evac >= 1.1:
                            for qs in range(4):
                                v.tensor_copy(OtF[:, qs * 132:qs * 132 + 129],
                                              Ops[qs][:, 0:129]
                                              ).then_inc(S["vvA"], 1)
                                nva += 1
                            v.wait_ge(S["vvA"], nva)
                        if _evac >= 1.2:
                            for qs in range(4):
                                v.reciprocal(rcA[:, qs:qs + 1],
                                             OtF[:, qs * 132 + 128:qs * 132 + 129]
                                             ).then_inc(S["vvA"], 1)
                                nva += 1
                            v.wait_ge(S["vvA"], nva)
                        if _evac >= 2:
                            for qs in range(4):
                                ts = v.tensor_scalar_mul(
                                    Otst[:, (j % 2) * 512 + qs * P:
                                         (j % 2) * 512 + qs * P + P],
                                    OtF[:, qs * 132:qs * 132 + P],
                                    rcA[:, qs:qs + 1])
                            ts.then_inc(S["vOev"], 1)
                        else:
                            v.nop().then_inc(S["vOev"], 1)

        if ph("o"):
         with contextlib.ExitStack() as oc_:
            eo = oc_.enter_context
            Otile = eo(nc.sbuf_tensor("Otile", [P, 2 * D], BF16))
            XtO = eo(nc.sbuf_tensor("XtO", [P, 2 * D], F32))
            x2t = eo(nc.sbuf_tensor("x2t", [P, 2 * D], F32))
            h2b = eo(nc.sbuf_tensor("h2b", [P, 2 * D], BF16))
            OfmT = eo(nc.sbuf_tensor("OfmT", [P, 2 * D], BF16))
            h2T = eo(nc.sbuf_tensor("h2T", [P, 2 * D], BF16))
            lnstO = eo(nc.sbuf_tensor("lnstO", [P, 16], F32))
            lgsb = eo(nc.sbuf_tensor("lgsb", [P, 8], F32))
            gtmp = eo(nc.sbuf_tensor("gtmp", [P, 24], F32))
            tpO = [eo(nc.psum_tensor("tpO%d" % i_, [P, P], BF16))
                   for i_ in range(3)]
            attn_ps = [eo(nc.psum_tensor("attnA", [P, 1024], F32)),
                       eo(nc.psum_tensor("attnB", [P, 1024], F32))]
            lgps = eo(nc.psum_tensor("lgps", [P, 8], F32))

            tsched = []
            for t in range(NT):
                tsched.append(("O", t))
                if t >= 1:
                    tsched.append(("H", t - 1))
            tsched.append(("H", NT - 1))
            kbase = {}
            _kk = 0
            for e_ in tsched:
                kbase[e_] = _kk
                _kk += ND

            vvo_ln = []
            vvo_gate = [None] * NT

            def aaO_rstd(t):
                return 2 if t == 0 else 3 * t + 1

            def aaO_gate(u):
                return 3 * u + 5 if u < NT - 1 else 3 * NT

            with nc.Block() as blk:
                @blk.sync
                def _(s):
                    s.wait_ge(S["sOw"], 16 * H * NTC)
                    for t in range(NT):
                        b = t % 2
                        if t >= 2:
                            s.wait_ge(S["pT1"], kbase[("O", t - 2)] + ND)
                            s.wait_ge(S["vEvO"], 2 * (t - 2) + 2)
                        s.dma_start(
                            Otile[:, b * D:b * D + D],
                            bass.AP(otok, t * P * D, [[D, P], [1, D]]),
                        ).then_inc(S["sOl"], 16)
                        s.dma_start(
                            XtO[:, b * D:b * D + D],
                            bass.AP(x_in, t * P * D, [[D, P], [1, D]]),
                        ).then_inc(S["sOl"], 16)

                def emit_h2(tn, u):
                    ub = u % 2
                    tn.wait_ge(S["vH2"], u + 1)
                    k0 = kbase[("H", u)]
                    for c in range(ND):
                        k = k0 + c
                        if k >= 3:
                            tn.wait_ge(S["aCp1"], k - 2)
                        tn.transpose(tpO[k % 3][:, :],
                                     h2b[:, ub * D + c * P:ub * D + c * P + P],
                                     ident[:, :]).then_inc(S["pT1"], 1)
                    tn.wait_ge(S["aCp1"], k0 + ND)
                    if u >= 1:
                        tn.wait_ge(S["vGt"], u)
                    for d in range(ND):
                        mm = tn.matmul(lgps[:, :],
                                       h2T[:, ub * D + d * P:ub * D + d * P + P],
                                       wg_sb[:, d * 8:d * 8 + 8],
                                       start=(d == 0), stop=(d == ND - 1))
                    mm.then_inc(S["pLg"], 1)

                @blk.tensor
                def _(tn):
                    tn.wait_ge(S["sWB1"], 128)
                    for t in range(NT):
                        b = t % 2
                        tn.wait_ge(S["sOl"], 32 * (t + 1))
                        k0 = kbase[("O", t)]
                        for c in range(ND):
                            k = k0 + c
                            if k >= 3:
                                tn.wait_ge(S["aCp1"], k - 2)
                            tn.transpose(tpO[k % 3][:, :],
                                         Otile[:, b * D + c * P:b * D + c * P + P],
                                         ident[:, :]).then_inc(S["pT1"], 1)
                        for half in range(2):
                            if half == 0:
                                tn.wait_ge(S["aCp1"], k0 + ND)
                            if t >= 1:
                                tn.wait_ge(S["vEvO"], 2 * (t - 1) + half + 1)
                            for d in range(ND):
                                for sl in range(2):
                                    mm = tn.matmul(
                                        attn_ps[half][:, sl * 512:sl * 512 + 512],
                                        OfmT[:, b * D + d * P:b * D + d * P + P],
                                        WB1[:, d * D + half * 1024 + sl * 512:
                                            d * D + half * 1024 + sl * 512 + 512],
                                        start=(d == 0), stop=(d == ND - 1))
                            mm.then_inc(S["pWo"], 1)
                        if t >= 1:
                            emit_h2(tn, t - 1)
                    emit_h2(tn, NT - 1)

                def emit_gating(v, u, nvo):
                    v.wait_ge(S["pLg"], u + 1)
                    v.tensor_copy(lgsb[:, :], lgps[:, :]).then_inc(S["vvO"], 1)
                    nvo += 1
                    v.wait_ge(S["vvO"], nvo)
                    v.nop().then_inc(S["vGt"], 1)
                    v.reduce_max(gtmp[:, 0:1], lgsb[:, :], axis=AX
                                 ).then_inc(S["vvO"], 1)
                    nvo += 1
                    v.wait_ge(S["vvO"], nvo)
                    v.tensor_scalar(gtmp[:, 8:16], lgsb[:, :],
                                    gtmp[:, 0:1], None, op0=ALU.subtract
                                    ).then_inc(S["vvO"], 1)
                    nvo += 1
                    v.wait_ge(S["vvO"], nvo)
                    v.tensor_scalar(gtmp[:, 16:24], gtmp[:, 8:16], 0.0,
                                    None, op0=ALU.is_ge
                                    ).then_inc(S["vvO"], 1)
                    nvo += 1
                    v.wait_ge(S["vvO"], nvo)
                    v.scalar_tensor_tensor(gtmp[:, 8:16], gtmp[:, 16:24],
                                           -1e30, gtmp[:, 8:16],
                                           op0=ALU.mult, op1=ALU.add
                                           ).then_inc(S["vvO"], 1)
                    nvo += 1
                    v.wait_ge(S["vvO"], nvo)
                    v.reduce_max(gtmp[:, 1:2], gtmp[:, 8:16], axis=AX
                                 ).then_inc(S["vvO"], 1)
                    nvo += 1
                    vvo_gate[u] = nvo
                    v.wait_ge(S["aaO"], aaO_gate(u))
                    v.tensor_scalar_add(gtmp[:, 3:4], gtmp[:, 2:3], 1.0
                                        ).then_inc(S["vvO"], 1)
                    nvo += 1
                    v.wait_ge(S["vvO"], nvo)
                    v.reciprocal(gtmp[:, 4:5], gtmp[:, 3:4]
                                 ).then_inc(S["vvO"], 1)
                    nvo += 1
                    v.wait_ge(S["vvO"], nvo)
                    v.tensor_mul(gtmp[:, 5:6], gtmp[:, 2:3], gtmp[:, 4:5]
                                 ).then_inc(S["vvO"], 1)
                    nvo += 1
                    v.wait_ge(S["vvO"], nvo)
                    v.tensor_add(rw_all[:, u:u + 1], gtmp[:, 4:5],
                                 gtmp[:, 5:6])
                    return nvo

                @blk.vector
                def _(v):
                    nvo = 0
                    for t in range(NT):
                        b = t % 2
                        for half in range(2):
                            v.wait_ge(S["pWo"], 2 * t + half + 1)
                            if half == 0 and t >= 2:
                                v.wait_ge(S["sX2w"], 16 * (t - 1))
                            add = v.tensor_add(
                                x2t[:, b * D + half * 1024:b * D + half * 1024 + 1024],
                                XtO[:, b * D + half * 1024:b * D + half * 1024 + 1024],
                                attn_ps[half][:, :])
                            if half == 0:
                                add.then_inc(S["vEvO"], 1)
                        x2a = x2t[:, b * D:b * D + D]
                        v.scalar_tensor_tensor(
                            OfmT[:, b * D:b * D + D], x2a, 1.0, x2a,
                            op0=ALU.mult, op1=ALU.mult,
                            accum_out=lnstO[:, 1:2]).then_inc(S["vvO"], 1)
                        v.reduce_sum(lnstO[:, 0:1], x2a, axis=AX
                                     ).then_inc(S["vvO"], 1)
                        nvo += 2
                        v.wait_ge(S["vvO"], nvo)
                        v.nop().then_inc(S["vEvO"], 1)
                        v.tensor_scalar_mul(lnstO[:, 2:3], lnstO[:, 0:1], 1.0 / D
                                            ).then_inc(S["vvO"], 1)
                        v.tensor_scalar(lnstO[:, 3:4], lnstO[:, 1:2], 1.0 / D,
                                        EPS, op0=ALU.mult, op1=ALU.add
                                        ).then_inc(S["vvO"], 1)
                        nvo += 2
                        v.wait_ge(S["vvO"], nvo)
                        v.tensor_mul(lnstO[:, 4:5], lnstO[:, 2:3],
                                     lnstO[:, 2:3]).then_inc(S["vvO"], 1)
                        nvo += 1
                        v.wait_ge(S["vvO"], nvo)
                        v.tensor_sub(lnstO[:, 5:6], lnstO[:, 3:4],
                                     lnstO[:, 4:5]).then_inc(S["vvO"], 1)
                        nvo += 1
                        vvo_ln.append(nvo)
                        v.wait_ge(S["aaO"], aaO_rstd(t))
                        v.tensor_scalar(lnstO[:, 8:9], lnstO[:, 2:3],
                                        lnstO[:, 7:8], -1.0,
                                        op0=ALU.mult, op1=ALU.mult
                                        ).then_inc(S["vvO"], 1)
                        nvo += 1
                        v.wait_ge(S["vvO"], nvo)
                        if t >= 2:
                            v.wait_ge(S["pT1"], kbase[("H", t - 2)] + ND)
                        v.tensor_scalar(h2b[:, b * D:b * D + D], x2a,
                                        lnstO[:, 7:8], lnstO[:, 8:9],
                                        op0=ALU.mult, op1=ALU.add
                                        ).then_inc(S["vH2"], 1)
                        if t >= 1:
                            nvo = emit_gating(v, t - 1, nvo)
                    emit_gating(v, NT - 1, nvo)

                def emit_h2_copies(sc, u):
                    ub = u % 2
                    if u >= 2:
                        sc.wait_ge(S["sH2w"], 16 * (u - 1))
                        sc.wait_ge(S["pLg"], u - 1)
                    k0 = kbase[("H", u)]
                    for c in range(ND):
                        k = k0 + c
                        sc.wait_ge(S["pT1"], k + 1)
                        sc.copy(h2T[:, ub * D + c * P:ub * D + c * P + P],
                                tpO[k % 3][:, :]).then_inc(S["aCp1"], 1)
                    sc.wait_ge(S["vvO"], vvo_gate[u])
                    sc.activation(gtmp[:, 2:3], gtmp[:, 1:2], AF.Exp
                                  ).then_inc(S["aaO"], 1)

                @blk.scalar
                def _(sc):
                    for t in range(NT):
                        b = t % 2
                        if t >= 2:
                            sc.wait_ge(S["pWo"], 2 * (t - 2) + 2)
                        k0 = kbase[("O", t)]
                        for c in range(ND):
                            k = k0 + c
                            sc.wait_ge(S["pT1"], k + 1)
                            sc.copy(OfmT[:, b * D + c * P:b * D + c * P + P],
                                    tpO[k % 3][:, :]).then_inc(S["aCp1"], 1)
                        sc.wait_ge(S["vvO"], vvo_ln[t])
                        sc.activation(lnstO[:, 6:7], lnstO[:, 5:6], AF.Ln
                                      ).then_inc(S["aaO"], 1)
                        sc.wait_ge(S["aaO"], aaO_rstd(t) - 1)
                        sc.activation(lnstO[:, 7:8], lnstO[:, 6:7], AF.Exp,
                                      scale=-0.5).then_inc(S["aaO"], 1)
                        if t >= 1:
                            emit_h2_copies(sc, t - 1)
                    emit_h2_copies(sc, NT - 1)

                @blk.gpsimd
                def _(g):
                    for t in range(NT):
                        b = t % 2
                        g.wait_ge(S["vEvO"], 2 * t + 2)
                        g.dma_start(
                            bass.AP(x2d, t * P * D, [[D, P], [1, D]]),
                            x2t[:, b * D:b * D + D]).then_inc(S["sX2w"], 16)
                        g.wait_ge(S["aCp1"], kbase[("H", t)] + ND)
                        g.dma_start(
                            bass.AP(h2d, t * P, [[L, P], [P * L, ND], [1, P]]),
                            h2T[:, :].rearrange("p (b c m) -> p b c m",
                                                b=2, c=ND)[:, b],
                        ).then_inc(S["sH2w"], 16)

        if ph("f1"):
         with contextlib.ExitStack() as fc_:
            ef = fc_.enter_context
            h2cF = ef(nc.sbuf_tensor("h2cF", [P, 2 * ND * 512], BF16))
            upst = ef(nc.sbuf_tensor("upst", [P, 2 * ND * 512], BF16))
            upps = [ef(nc.psum_tensor("upA", [P, 512], F32)),
                    ef(nc.psum_tensor("upB", [P, 512], F32))]

            with nc.Block() as blk:
                @blk.sync
                def _(s):
                    s.wait_ge(S["sH2w"], 16 * NT)
                    for tc in range(NTC):
                        if tc >= 2:
                            s.wait_ge(S["pF1"], ND * (tc - 1))
                        b = tc % 2
                        s.dma_start(
                            h2cF[:, :].rearrange("p (b c m) -> p b c m",
                                                 b=2, c=ND)[:, b],
                            bass.AP(h2d, tc * 512,
                                    [[L, P], [P * L, ND], [1, 512]]),
                        ).then_inc(S["sh2F"], 16)

                @blk.tensor
                def _(tn):
                    tn.wait_ge(S["sWB0"], 192)
                    for tc in range(NTC):
                        tn.wait_ge(S["sh2F"], 16 * (tc + 1))
                        b = tc % 2
                        for ee in range(ND):
                            i = tc * ND + ee
                            if i >= 2:
                                tn.wait_ge(S["aRl"], i - 1)
                            for d in range(ND):
                                mm = tn.matmul(
                                    upps[i % 2][:, :],
                                    WB0[:, d * D + ee * P:d * D + ee * P + P],
                                    h2cF[:, b * ND * 512 + d * 512:
                                         b * ND * 512 + d * 512 + 512],
                                    start=(d == 0), stop=(d == ND - 1))
                            mm.then_inc(S["pF1"], 1)

                @blk.scalar
                def _(sc):
                    for tc in range(NTC):
                        b = tc % 2
                        if tc >= 2:
                            sc.wait_ge(S["sUw"], 16 * (tc - 1))
                        for ee in range(ND):
                            i = tc * ND + ee
                            sc.wait_ge(S["pF1"], i + 1)
                            sc.activation(
                                upst[:, b * ND * 512 + ee * 512:
                                     b * ND * 512 + ee * 512 + 512],
                                upps[i % 2][:, :], AF.Relu).then_inc(S["aRl"], 1)

                @blk.gpsimd
                def _(g):
                    wload(g, WB1, wdownT, D, 0, D, S["sWB1"])
                    for tc in range(NTC):
                        b = tc % 2
                        g.wait_ge(S["aRl"], ND * (tc + 1))
                        g.dma_start(
                            bass.AP(upd, tc * 512,
                                    [[L, P], [P * L, ND], [1, 512]]),
                            upst[:, :].rearrange("p (b c m) -> p b c m",
                                                 b=2, c=ND)[:, b],
                        ).then_inc(S["sUw"], 16)

        if ph("f23"):
         with contextlib.ExitStack() as gc_:
            eg = gc_.enter_context
            uptile = eg(nc.sbuf_tensor("uptile", [P, 2 * D], BF16))
            h2tile = eg(nc.sbuf_tensor("h2tile", [P, 2 * D], BF16))
            x2tile = eg(nc.sbuf_tensor("x2tile", [P, 2 * D], F32))
            outt = eg(nc.sbuf_tensor("outt", [P, 2 * D], F32))
            ftmp = eg(nc.sbuf_tensor("ftmp", [P, 1024], F32))
            sh_ps = [eg(nc.psum_tensor("shA", [P, 1024], F32)),
                     eg(nc.psum_tensor("shB", [P, 1024], F32))]
            ex_ps = [eg(nc.psum_tensor("exA", [P, 1024], F32)),
                     eg(nc.psum_tensor("exB", [P, 1024], F32))]

            with nc.Block() as blk:
                @blk.gpsimd
                def _(g):
                    wload(g, WB0, wexp, D, 0, D, S["sWB0"])

                @blk.sync
                def _(s):
                    s.wait_ge(S["sUw"], 16 * NTC)
                    s.wait_ge(S["sX2w"], 16 * NT)
                    for t in range(NT):
                        b = t % 2
                        if t >= 2:
                            s.wait_ge(S["vF23"], 2 * (t - 2) + 2)
                        s.dma_start(
                            uptile[:, :].rearrange("p (b c m) -> p b c m",
                                                   b=2, c=ND)[:, b],
                            bass.AP(upd, t * P,
                                    [[L, P], [P * L, ND], [1, P]]),
                        ).then_inc(S["sF3l"], 16)
                        s.dma_start(
                            h2tile[:, :].rearrange("p (b c m) -> p b c m",
                                                   b=2, c=ND)[:, b],
                            bass.AP(h2d, t * P,
                                    [[L, P], [P * L, ND], [1, P]]),
                        ).then_inc(S["sF3l"], 16)
                        s.dma_start(
                            x2tile[:, b * D:b * D + D],
                            bass.AP(x2d, t * P * D, [[D, P], [1, D]]),
                        ).then_inc(S["sF3l"], 16)

                @blk.tensor
                def _(tn):
                    tn.wait_ge(S["sWB0"], 256)
                    for t in range(NT):
                        b = t % 2
                        tn.wait_ge(S["sF3l"], 48 * (t + 1))
                        for half in range(2):
                            if t >= 1:
                                tn.wait_ge(S["vF23"], 2 * (t - 1) + half + 1)
                            for sl in range(2):
                                for o in range(ND):
                                    tn.matmul(
                                        sh_ps[half][:, sl * 512:sl * 512 + 512],
                                        uptile[:, b * D + o * P:b * D + o * P + P],
                                        WB1[:, o * D + half * 1024 + sl * 512:
                                            o * D + half * 1024 + sl * 512 + 512],
                                        start=(o == 0), stop=(o == ND - 1))
                            for sl in range(2):
                                for d in range(ND):
                                    mm = tn.matmul(
                                        ex_ps[half][:, sl * 512:sl * 512 + 512],
                                        h2tile[:, b * D + d * P:b * D + d * P + P],
                                        WB0[:, d * D + half * 1024 + sl * 512:
                                            d * D + half * 1024 + sl * 512 + 512],
                                        start=(d == 0), stop=(d == ND - 1))
                            mm.then_inc(S["pF2h"], 1)

                @blk.vector
                def _(v):
                    for t in range(NT):
                        b = t % 2
                        for half in range(2):
                            v.wait_ge(S["pF2h"], 2 * t + half + 1)
                            if half == 0 and t >= 2:
                                v.wait_ge(S["sOutW"], 16 * (t - 1))
                            v.scalar_tensor_tensor(
                                ftmp[:, :], ex_ps[half][:, :],
                                rw_all[:, t:t + 1],
                                x2tile[:, b * D + half * 1024:
                                       b * D + half * 1024 + 1024],
                                op0=ALU.mult, op1=ALU.add)
                            v.tensor_add(
                                outt[:, b * D + half * 1024:
                                     b * D + half * 1024 + 1024],
                                ftmp[:, :], sh_ps[half][:, :]
                            ).then_inc(S["vF23"], 1)

                @blk.scalar
                def _(sc):
                    for t in range(NT):
                        b = t % 2
                        sc.wait_ge(S["vF23"], 2 * t + 2)
                        sc.dma_start(
                            bass.AP(out, t * P * D, [[D, P], [1, D]]),
                            outt[:, b * D:b * D + D]).then_inc(S["sOutW"], 16)
                    sc.wait_ge(S["sOutW"], 16 * NT)

        if tap_outs:
            with nc.Block() as blk:
                @blk.sync
                def _(s):
                    for sem, tot in [("sH1w", 16 * NT), ("sQw", 16 * H * NTC),
                                     ("sKw", 16 * H * NTC),
                                     ("sVw", 16 * NTC * 16),
                                     ("sOw", 16 * H * NTC), ("sX2w", 16 * NT),
                                     ("sH2w", 16 * NT), ("sUw", 16 * NTC)]:
                        gate = {"sH1w": "ln1", "sQw": "q", "sKw": "k",
                                "sVw": "v", "sOw": "attn", "sX2w": "o",
                                "sH2w": "o", "sUw": "f1"}[sem]
                        if ph(gate):
                            s.wait_ge(S[sem], tot)
                    ntap = 0
                    for dst, src in tap_outs:
                        rows, cols = src.shape
                        s.dma_start(
                            bass.AP(dst, 0, [[cols, rows], [1, cols]]),
                            bass.AP(src, 0, [[cols, rows], [1, cols]]),
                        ).then_inc(S["sTap"], 16)
                        ntap += 1
                    s.wait_ge(S["sTap"], 16 * ntap)

    return nc


def host_inputs(L0_attn_x, L0_attn_W_qkv, L0_attn_W_o, L0_ffn_W_up,
                L0_ffn_W_down, L0_ffn_W_gate, L0_ffn_W_expert):
    bf16 = ml_dtypes.bfloat16
    x = np.ascontiguousarray(np.asarray(L0_attn_x, dtype=np.float32))
    wqkv = np.asarray(L0_attn_W_qkv, dtype=np.float32)
    pp = np.concatenate([np.arange(0, P, 2), np.arange(1, P, 2)])
    rows = np.arange(3 * D)
    for sec in (0, D):
        for h in range(H):
            base = sec + h * P
            rows[base:base + P] = base + pp
    wqkvT = np.ascontiguousarray(wqkv[rows].T)
    woT = np.ascontiguousarray(np.asarray(L0_attn_W_o, dtype=np.float32).T)
    wupT = np.ascontiguousarray(np.asarray(L0_ffn_W_up, dtype=np.float32).T)
    wdownT = np.ascontiguousarray(np.asarray(L0_ffn_W_down, dtype=np.float32).T)
    wexp = np.ascontiguousarray(np.asarray(L0_ffn_W_expert, dtype=np.float32))
    wgT = np.ascontiguousarray(np.asarray(L0_ffn_W_gate, dtype=np.float32).T)

    inv = (1.0 / (10000.0 ** (np.arange(0, P, 2, dtype=np.float32) / P))
           ).astype(np.float32)
    ang = np.arange(L, dtype=np.float32)[:, None] * inv[None, :]
    cosv = np.cos(ang).T.astype(np.float32)
    sinv = np.sin(ang).T.astype(np.float32)
    costab = np.ascontiguousarray(np.vstack([cosv, cosv]))
    nsintab = np.ascontiguousarray(np.vstack([-sinv, sinv]))

    identv = np.eye(P, dtype=bf16)
    masks = np.zeros((P, 4 * 512), dtype=np.float32)
    for j in range(4):
        masks[:, j * 512:(j + 1) * 512] = (
            (np.arange(512)[None, :] - P * j) >= np.arange(P)[:, None]
        ).astype(np.float32)
    masksv = masks.astype(bf16)

    shared = {"wqkvT": wqkvT, "woT": woT, "wupT": wupT, "wdownT": wdownT,
              "wexp": wexp, "wgT": wgT, "costab": costab, "nsintab": nsintab,
              "identin": identv, "masksin": masksv}
    in_maps = []
    for b in range(NCORES):
        m = dict(shared)
        m["x"] = np.ascontiguousarray(x[b])
        in_maps.append(m)
    return in_maps


def kernel(**inputs):
    in_maps = host_inputs(**inputs)
    nc = build_nc()
    res = run_bass_kernel_spmd(nc, in_maps, core_ids=list(range(NCORES)))
    return np.stack([np.asarray(r["out"], dtype=np.float32)
                     for r in res.results], axis=0)
